# version 07
# baseline (speedup 1.0000x reference)
import numpy as np
import concourse.bass as bass
import concourse.mybir as mybir
from concourse.bass_utils import run_bass_kernel_spmd

F32 = mybir.dt.float32
BF16 = mybir.dt.bfloat16
AF = mybir.ActivationFunctionType
ALU = mybir.AluOpType
AX = mybir.AxisListType

D = 1024
T = 2048
NSEQ = 2
NE = 8
FF = 4096
SPAN = 1024
NSPAN = T // SPAN
FC = 1024
NFC = FF // FC
NFSUB = FC // 128
SHIFT = 30.0
EPS = 1e-5

DEBUG = False

ENGS = ("tensor", "scalar", "vector", "sync", "gpsimd")


class Sem:
    __slots__ = ("h", "name", "n")

    def __init__(self, handle, name):
        self.h = handle
        self.name = name
        self.n = 0


class B:

    def __init__(self, sems):
        self.q = {e: [] for e in ENGS}
        self.sems = sems
        self.pending = {e: None for e in ENGS}

    def barrier(self):
        snap = [(s, s.n) for s in self.sems]
        for e in ENGS:
            self.pending[e] = list(snap)

    def emit(self, eng, fn, waits=(), incs=()):
        for (s, v) in waits:
            assert v <= s.n, f"wait-on-future {s.name}: {v} > {s.n}"
        bar = self.pending[eng]
        self.pending[eng] = None
        out_vals = []
        for (s, step) in incs:
            s.n += step
            out_vals.append(s.n)
        waits = tuple((s, v) for (s, v) in waits if v > 0)
        incs = tuple(incs)

        def run(e, bar=bar, waits=waits, incs=incs, fn=fn):
            if bar is not None:
                for (s, v) in bar:
                    if v > 0:
                        e.wait_ge(s.h, v)
            for (s, v) in waits:
                e.wait_ge(s.h, v)
            ins = fn(e)
            for (s, step) in incs:
                ins.then_inc(s.h, step)

        self.q[eng].append(run)
        if len(out_vals) == 1:
            return out_vals[0]
        return out_vals


class DmaQ:

    def __init__(self, bld, eng, sems):
        self.bld = bld
        self.eng = eng
        self.sems = sems
        self.i = 0

    def dma(self, fn, waits=()):
        s = self.sems[self.i % len(self.sems)]
        self.i += 1
        w = list(waits) + [(s, s.n)]
        v = self.bld.emit(self.eng, fn, waits=w, incs=[(s, 16)])
        return (s, v)

    def all_done(self):
        return [(s, s.n) for s in self.sems]


def build_nc():
    nc = bass.Bass("TRN2", target_bir_lowering=False)

    x_hbm = nc.dram_tensor("x", [NSEQ, T, D], F32, kind="ExternalInput").ap()
    qt_h = nc.dram_tensor("qt", [D, D], F32, kind="ExternalInput").ap()
    kt_h = nc.dram_tensor("kt", [D, D], F32, kind="ExternalInput").ap()
    vt_h = nc.dram_tensor("vt", [D, D], F32, kind="ExternalInput").ap()
    ot_h = nc.dram_tensor("ot", [D, D], F32, kind="ExternalInput").ap()
    rwt_h = nc.dram_tensor("rwt", [D, NE], F32, kind="ExternalInput").ap()
    w1t_h = nc.dram_tensor("w1t", [NE, D, FF], F32, kind="ExternalInput").ap()
    w2t_h = nc.dram_tensor("w2t", [NE, FF, D], F32, kind="ExternalInput").ap()
    b1r_h = nc.dram_tensor("b1r", [128, NE, 32], F32, kind="ExternalInput").ap()
    b2_h = nc.dram_tensor("b2", [NE, D], F32, kind="ExternalInput").ap()
    id_h = nc.dram_tensor("ident", [128, 128], F32, kind="ExternalInput").ap()
    mk_h = nc.dram_tensor("mask", [128, 128], F32, kind="ExternalInput").ap()
    sel_h = nc.dram_tensor("sel2", [16, 1024], F32, kind="ExternalInput").ap()
    shf_h = nc.dram_tensor("shiftc", [128, 1], F32, kind="ExternalInput").ap()
    y_hbm = nc.dram_tensor("y", [NSEQ, T, D], F32, kind="ExternalOutput").ap()
    if DEBUG:
        dbg_xn = nc.dram_tensor("dbg_xn", [NSEQ, 128, 16384], BF16, kind="ExternalOutput").ap()
        dbg_g = nc.dram_tensor("dbg_g", [NSEQ, 128, 16384], BF16, kind="ExternalOutput").ap()
        dbg_z = nc.dram_tensor("dbg_z", [NSEQ, 128, 16384], BF16, kind="ExternalOutput").ap()
        dbg_wt = nc.dram_tensor("dbg_wt", [NSEQ, NSPAN, 128, 64], F32, kind="ExternalOutput").ap()
        dbg_y = nc.dram_tensor("dbg_y", [NSEQ, NSPAN, 128, 8192], F32, kind="ExternalOutput").ap()

    from contextlib import ExitStack
    es = ExitStack()
    sb = lambda name, shape, dt: es.enter_context(nc.sbuf_tensor(name, shape, dt))
    ps = lambda name, shape, dt: es.enter_context(nc.psum_tensor(name, shape, dt))
    sem = lambda name: es.enter_context(nc.semaphore(name))
    with es:
        xn_fm = sb("xn_fm", [128, 16384], BF16)
        g_park = sb("g_park", [128, 16384], BF16)
        arena_b = sb("arena_b", [128, 50432], BF16)
        arena_f = sb("arena_f", [128, 8192], F32)
        ident_f = sb("ident_f", [128, 128], F32)
        mask_bf = sb("mask_bf", [128, 128], BF16)
        rwt_bf = sb("rwt_bf", [128, 64], BF16)
        b1_sb = sb("b1_sb", [128, 256], F32)
        b2_bf = sb("b2_bf", [8, 1024], BF16)
        wt_sp = sb("wt_sp", [128, 128], F32)
        wtt_sb = sb("wtt_sb", [8, 2048], BF16)
        rt_lg = sb("rt_lg", [128, 8], F32)
        rt_rank = sb("rt_rank", [128, 8], F32)
        rt_cmp = sb("rt_cmp", [128, 8], F32)
        rt_E = sb("rt_E", [128, 8], F32)
        rt_Em = sb("rt_Em", [128, 8], F32)
        rt_S = sb("rt_S", [128, 1], F32)
        rt_d = sb("rt_d", [128, 1], F32)
        rt_rcp = sb("rt_rcp", [128, 1], F32)
        st_sum = sb("st_sum", [128, 1], F32)
        st_ssq = sb("st_ssq", [128, 1], F32)
        st_ngm = sb("st_ngm", [128, 1], F32)
        st_var = sb("st_var", [128, 1], F32)
        st_tmp = sb("st_tmp", [128, 1], F32)
        st_rstd = sb("st_rstd", [128, 1], F32)
        shift_sb = sb("shift_sb", [128, 1], F32)
        psA = ps("psA", [128, 2048], F32)
        psB = ps("psB", [128, 2048], F32)
        _sg = [sem(f"sG{i}") for i in range(4)]
        _ss = [sem(f"sS{i}") for i in range(4)]
        _sPE = sem("sPE"); _sACT = sem("sACT"); _sDVE = sem("sDVE")
        block = es.enter_context(nc.Block())

        GQS = [Sem(h, f"G{i}") for i, h in enumerate(_sg)]
        SQS = [Sem(h, f"S{i}") for i, h in enumerate(_ss)]
        PE = Sem(_sPE, "PE")
        ACT = Sem(_sACT, "ACT")
        DVE = Sem(_sDVE, "DVE")
        bld = B(GQS + SQS + [PE, ACT, DVE])
        GQ = DmaQ(bld, "gpsimd", GQS)
        SQ = DmaQ(bld, "sync", SQS)

        def cb(col_off, parts, cols):
            return arena_b[0:parts, col_off: col_off + cols]

        def cf(col_off, parts, cols):
            return arena_f[0:parts, col_off: col_off + cols]

        y_acc = cf(0, 128, 8192)
        h_sb = cb(0, 128, 8192)
        w1r = cb(8192, 128, 8192)
        w2c = cb(16384, 128, 16384)
        x_ln = cf(0, 128, 2048)
        xn_tmp = cf(2048, 128, 2048)
        ln_sq = cf(4096, 128, 1024)
        v_aug = cb(0, 128, 16640)
        z_park = cb(16640, 128, 16384)
        vt_ot = cb(33024, 128, 8192)
        qk_w = cb(33024, 128, 4096)
        q_hp = cb(37120, 128, 2048)
        k_hp = cb(39168, 128, 2048)
        attnT = cb(41216, 128, 4096)
        recip_bf = cb(45312, 16, 2048)
        sel2_bf = cb(47360, 16, 1024)
        zstage = cb(48384, 64, 2048)
        sums = cf(0, 16, 2048)
        srstage = cf(2048, 128, 2048)
        x_att = cf(4096, 128, 2048)
        out_st = cf(6144, 128, 2048)

        v_ident = SQ.dma(lambda s: s.dma_start(ident_f[:, :], id_h[:, :]))
        v_mask = GQ.dma(lambda g: g.dma_start(mask_bf[:, :], mk_h[:, :]))
        v_sel = GQ.dma(lambda g: g.dma_start(sel2_bf[:, :], sel_h[:, :]))
        v_rwt = GQ.dma(
            lambda g: g.dma_start(rwt_bf.rearrange("p (cc e) -> p cc e", e=8), rwt_h.rearrange("(cc p) e -> p cc e", p=128)))
        v_b2 = GQ.dma(lambda g: g.dma_start(b2_bf[:, :], b2_h[:, :]))
        v_b1 = SQ.dma(
            lambda s: s.dma_start(b1_sb[:, :], b1r_h.rearrange("p e f -> p (e f)")))
        v_shf = SQ.dma(lambda s: s.dma_start(shift_sb[:, :], shf_h[:, :]))
        consts_done_X = v_shf

        def vdrain():
            bld.emit("vector", lambda v: v.drain())

        trk = {}

        def phase_ln(seq):
            xdma = [0] * 16
            trevac = {}
            xnw = [0] * 16
            lastT = [0] * 16
            rstdv = [0] * 16

            def dma_x(tc):
                par = tc % 2
                w = []
                if tc >= 2:
                    w = [(DVE, xnw[tc - 2])]
                src = x_hbm[seq, tc * 128:(tc + 1) * 128, :]
                dst = x_ln[:, par * 1024:(par + 1) * 1024]
                xdma[tc] = SQ.dma(lambda s, d=dst, sr=src: s.dma_start(d, sr), waits=w)

            dma_x(0)
            dma_x(1)
            for tc in range(16):
                par = tc % 2
                xt = x_ln[:, par * 1024:(par + 1) * 1024]
                xnt = xn_tmp[:, par * 1024:(par + 1) * 1024]
                bld.emit("vector",
                         lambda v, xt=xt: v.reduce_sum(st_sum[:, :], xt, axis=AX.X),
                         waits=[xdma[tc], (ACT, rstdv[tc - 1] if tc >= 1 else 0)])
                bld.emit("vector", lambda v, xt=xt: v.tensor_tensor(ln_sq[:, :], xt, xt, ALU.mult))
                vdrain()
                bld.emit("vector", lambda v: v.reduce_sum(st_ssq[:, :], ln_sq[:, :], axis=AX.X))
                bld.emit("vector", lambda v: v.tensor_scalar(st_ngm[:, :], st_sum[:, :], -1.0 / D, None, ALU.mult))
                vdrain()
                bld.emit("vector", lambda v: v.tensor_scalar(st_tmp[:, :], st_ssq[:, :], 1.0 / D, None, ALU.mult))
                vdrain()
                bld.emit("vector", lambda v: v.tensor_tensor(st_var[:, :], st_ngm[:, :], st_ngm[:, :], ALU.mult))
                vdrain()
                bld.emit("vector", lambda v: v.tensor_sub(st_var[:, :], st_tmp[:, :], st_var[:, :]))
                vdrain()
                vvar = bld.emit("vector", lambda v: v.tensor_scalar(st_var[:, :], st_var[:, :], EPS, None, ALU.add),
                                incs=[(DVE, 1)])
                lnv = bld.emit("scalar", lambda s: s.activation(st_tmp[:, :], st_var[:, :], AF.Ln),
                               waits=[(DVE, vvar)], incs=[(ACT, 1)])
                vml = bld.emit("vector", lambda v: v.tensor_scalar(st_tmp[:, :], st_tmp[:, :], -0.5, None, ALU.mult),
                               waits=[(ACT, lnv)], incs=[(DVE, 1)])
                rstdv[tc] = bld.emit(
                    "scalar", lambda s: s.activation(st_rstd[:, :], st_tmp[:, :], AF.Exp),
                    waits=[(DVE, vml)], incs=[(ACT, 1)])
                xnw[tc] = bld.emit(
                    "vector",
                    lambda v, xt=xt, xnt=xnt: v.tensor_scalar(xnt, xt, st_ngm[:, 0:1], st_rstd[:, 0:1], ALU.add, ALU.mult),
                    waits=[(ACT, rstdv[tc]), (PE, lastT[tc - 2] if tc >= 2 else 0)],
                    incs=[(DVE, 1)])
                for cc in range(8):
                    bank = (tc * 8 + cc) % 4
                    tdst = psA[:, bank * 512: bank * 512 + 128]
                    tsrc = xnt[:, cc * 128:(cc + 1) * 128]
                    lastT[tc] = bld.emit(
                        "tensor",
                        lambda t, d=tdst, s=tsrc: t.transpose(d, s, ident_f[:, :]),
                        waits=[(DVE, max(xnw[tc], trevac.get(bank, 0))), v_ident],
                        incs=[(PE, 1)])
                    edst = xn_fm[:, cc * 2048 + tc * 128: cc * 2048 + (tc + 1) * 128]
                    trevac[bank] = bld.emit(
                        "vector",
                        lambda v, d=edst, s=tdst: v.tensor_copy(d, s),
                        waits=[(PE, lastT[tc])],
                        incs=[(DVE, 1)])
                trk[("xn_tc", seq, tc)] = max(trevac.values())
                if tc + 2 < 16:
                    dma_x(tc + 2)
                if tc == 7:
                    emit_router(seq, 0)
                if tc == 15:
                    emit_router(seq, 1)
            trk[("xn_done", seq)] = max(trevac.values())
            if DEBUG:
                SQ.dma(lambda s, seq=seq: s.dma_start(dbg_xn[seq], xn_fm[:, :]),
                       waits=[(DVE, trk[("xn_done", seq)])])

        def emit_router(seq, span):
            sp0 = span * SPAN
            dveb_snap = max(DVE.n and trk.get(("router_psB_free", seq), 0), trk.get(("router_psB_free", seq), 0))
            lg_use = [0, 0]
            exp_done = [0, 0]
            wt_done = [0] * 8
            if True:
                for tc in range(8):
                    par = tc % 2
                    lg = psB[:, par * 512: par * 512 + 8]
                    last = 0
                    for cc in range(8):
                        w = []
                        if cc == 0:
                            w = [(DVE, trk[("xn_tc", seq, span * 8 + tc)]), v_rwt, (DVE, lg_use[par]), (DVE, dveb_snap)]
                        last = bld.emit(
                            "tensor",
                            lambda t, lg=lg, cc=cc, tc=tc, sp0=sp0: t.matmul(
                                lg,
                                xn_fm[:, cc * 2048 + sp0 + tc * 128: cc * 2048 + sp0 + (tc + 1) * 128],
                                rwt_bf[:, cc * 8:(cc + 1) * 8],
                                start=(cc == 0), stop=(cc == 7)),
                            waits=w,
                            incs=([(PE, 1)] if cc == 7 else []))
                    lg_use[par] = bld.emit(
                        "vector", lambda v, lg=lg: v.tensor_copy(rt_lg[:, :], lg),
                        waits=[(PE, last), (ACT, exp_done[0]), (ACT, exp_done[1])],
                        incs=[(DVE, 1)])
                    vdrain()
                    bld.emit("vector", lambda v: v.tensor_scalar(rt_rank[:, :], rt_lg[:, :], rt_lg[:, 0:1], None, ALU.is_lt))
                    for j in range(1, 8):
                        vdrain()
                        bld.emit("vector", lambda v, j=j: v.tensor_scalar(rt_cmp[:, :], rt_lg[:, :], rt_lg[:, j:j + 1], None, ALU.is_lt))
                        vdrain()
                        bld.emit("vector", lambda v: v.tensor_add(rt_rank[:, :], rt_rank[:, :], rt_cmp[:, :]))
                    for j in range(0, 7):
                        vdrain()
                        bld.emit("vector", lambda v, j=j: v.tensor_scalar(
                            rt_cmp[:, j + 1:8], rt_lg[:, j + 1:8], rt_lg[:, j:j + 1], None, ALU.is_equal))
                        vdrain()
                        bld.emit("vector", lambda v, j=j: v.tensor_add(rt_rank[:, j + 1:8], rt_rank[:, j + 1:8], rt_cmp[:, j + 1:8]))
                    vdrain()
                    bld.emit("vector", lambda v: v.tensor_scalar(rt_cmp[:, :], rt_rank[:, :], 1.5, None, ALU.is_le))
                    vdrain()
                    exp_done[par] = bld.emit(
                        "scalar", lambda s: s.activation(rt_E[:, :], rt_lg[:, :], AF.Exp),
                        waits=[(DVE, lg_use[par])], incs=[(ACT, 1)])
                    bld.emit("vector", lambda v: v.reduce_sum(rt_S[:, :], rt_E[:, :], axis=AX.X),
                             waits=[(ACT, exp_done[par])])
                    bld.emit("vector", lambda v: v.tensor_mul(rt_Em[:, :], rt_E[:, :], rt_cmp[:, :]))
                    vdrain()
                    bld.emit("vector", lambda v: v.reduce_sum(rt_d[:, :], rt_Em[:, :], axis=AX.X))
                    vdrain()
                    bld.emit("vector", lambda v: v.scalar_tensor_tensor(rt_d[:, :], rt_S[:, :], 1e-9, rt_d[:, :], ALU.mult, ALU.add))
                    vdrain()
                    bld.emit("vector", lambda v: v.reciprocal(rt_rcp[:, :], rt_d[:, :]))
                    vdrain()
                    wt_done[tc] = bld.emit(
                        "vector", lambda v, tc=tc, span=span: v.tensor_scalar(
                            wt_sp[:, span * 64 + tc * 8: span * 64 + (tc + 1) * 8], rt_Em[:, :], rt_rcp[:, 0:1], None, ALU.mult),
                        incs=[(DVE, 1)])

                lg_done = max(lg_use)
                wtt_done = [0] * 8
                for tc in range(8):
                    slot = tc % 2
                    tdst = psB[0:8, (2 + slot) * 512: (2 + slot) * 512 + 128]
                    w = [(DVE, wt_done[tc])]
                    if tc == 0:
                        w.append((DVE, dveb_snap))
                    if tc >= 2:
                        w.append((DVE, wtt_done[tc - 2]))
                    wtt_pe = bld.emit(
                        "tensor",
                        lambda t, d=tdst, tc=tc, span=span: t.transpose(
                            d, wt_sp[:, span * 64 + tc * 8: span * 64 + (tc + 1) * 8], ident_f[:, :]),
                        waits=w, incs=[(PE, 1)])
                    wtt_done[tc] = bld.emit(
                        "vector",
                        lambda v, d=tdst, tc=tc, span=span: v.tensor_copy(
                            wtt_sb[0:8, span * 1024 + tc * 128: span * 1024 + (tc + 1) * 128], d),
                        waits=[(PE, wtt_pe)], incs=[(DVE, 1)])
            trk[("router", seq, span)] = (wt_done, wtt_done, max(lg_use))
            trk[("router_psB_free", seq)] = DVE.n

        def phase_ffn(seq):
            w1_dma = trk.setdefault("w1_dma", {})
            w2_dma = trk.setdefault("w2_dma", {})
            w1_free = trk.setdefault("w1_free", {})
            w2_free = trk.setdefault("w2_free", {})
            h_slot_free = trk.setdefault("h_slot_free", {})
            y_par_free = trk.setdefault("y_par_free", {})
            router_dve = DVE.n

            for span in range(NSPAN):
                sp0 = span * SPAN
                dve_snap = DVE.n
                wt_done, wtt_done, lg_done = trk[("router", seq, span)]
                xn_ok = trk[("xn_tc", seq, span * 8 + 7)] if span == 0 else trk[("xn_tc", seq, 15)]
                wtt_last = max(wtt_done)
                b2init_done = [0] * 8
                for tc in range(8):
                    par = tc % 2
                    last = 0
                    for ch in range(2):
                        w = []
                        if ch == 0:
                            w = [(DVE, max(router_dve, dve_snap)), v_b2]
                            if tc >= 2:
                                w.append((ACT, b2init_done[tc - 2]))
                        last = bld.emit(
                            "tensor",
                            lambda t, tc=tc, ch=ch, par=par, span=span: t.matmul(
                                psB[:, par * 1024 + ch * 512: par * 1024 + (ch + 1) * 512],
                                wtt_sb[0:8, span * 1024 + tc * 128: span * 1024 + (tc + 1) * 128],
                                b2_bf[0:8, ch * 512:(ch + 1) * 512],
                                start=True, stop=True),
                            waits=w,
                            incs=([(PE, 1)] if ch == 1 else []))
                    b2init_done[tc] = bld.emit(
                        "scalar",
                        lambda s, tc=tc, par=par: s.activation(
                            y_acc[:, tc * 1024:(tc + 1) * 1024], psB[:, par * 1024:(par + 1) * 1024], AF.Copy),
                        waits=[(PE, last)], incs=[(ACT, 1)])

                h_evac = {}
                stt_done = {}

                def dma_w1(e, fc, ch2):
                    idx = (e * NFC + fc) * 2 + ch2
                    slot = idx % 2
                    f0 = fc * FC + ch2 * 512
                    src = w1t_h[e, :, f0:f0 + 512].rearrange("(cc p) f -> p cc f", p=128)
                    dst = w1r[:, slot * 4096:(slot + 1) * 4096].rearrange("p (cc f) -> p cc f", f=512)
                    w = [(PE, w1_free.get(slot, 0))]
                    w1_dma[(e, fc, ch2)] = GQ.dma(lambda g, d=dst, s=src: g.dma_start(d, s), waits=w)

                def dma_w2(e, fc):
                    idx = e * NFC + fc
                    slot = idx % 2
                    src = w2t_h[e, fc * FC:(fc + 1) * FC, :].rearrange("(fs p) c -> p fs c", p=128)
                    dst = w2c[:, slot * 8192:(slot + 1) * 8192].rearrange("p (fs c) -> p fs c", c=1024)
                    w = [(PE, w2_free.get(slot, 0))]
                    w2_dma[(e, fc)] = GQ.dma(lambda g, d=dst, s=src: g.dma_start(d, s), waits=w)

                dma_w1(0, 0, 0)
                dma_w1(0, 0, 1)
                dma_w2(0, 0)

                for e in range(NE):
                    for fc in range(NFC):
                        idx = e * NFC + fc
                        w2slot = idx % 2
                        nxt = idx + 1
                        if nxt < NE * NFC:
                            en, fcn = nxt // NFC, nxt % NFC
                            dma_w2(en, fcn)
                        for ch2 in range(2):
                            w1slot = ((e * NFC + fc) * 2 + ch2) % 2
                            for fl in range(4):
                                fs = ch2 * 4 + fl
                                for th in range(2):
                                    pslot = (fs * 2 + th) % 4
                                    pdst = psA[:, pslot * 512:(pslot + 1) * 512]
                                    last = 0
                                    for cc in range(8):
                                        w = []
                                        if cc == 0:
                                            w = [w1_dma[(e, fc, ch2)],
                                                 (ACT, h_slot_free.get(pslot, 0)),
                                                 (DVE, max(xn_ok, lg_done, wtt_last))]
                                        last = bld.emit(
                                            "tensor",
                                            lambda t, pdst=pdst, w1slot=w1slot, cc=cc, fl=fl, th=th, sp0=sp0: t.matmul(
                                                pdst,
                                                w1r[:, w1slot * 4096 + cc * 512 + fl * 128: w1slot * 4096 + cc * 512 + (fl + 1) * 128],
                                                xn_fm[:, cc * 2048 + sp0 + th * 512: cc * 2048 + sp0 + (th + 1) * 512],
                                                start=(cc == 0), stop=(cc == 7)),
                                            waits=w,
                                            incs=([(PE, 1)] if cc == 7 else []))
                                    w1_free[w1slot] = last
                                    bia = b1_sb[:, e * 32 + fc * 8 + fs: e * 32 + fc * 8 + fs + 1]
                                    h_slot_free[pslot] = bld.emit(
                                        "scalar",
                                        lambda s, fs=fs, th=th, pdst=pdst, bia=bia: s.activation(
                                            h_sb[:, fs * 1024 + th * 512: fs * 1024 + (th + 1) * 512],
                                            pdst, AF.Relu, bias=bia),
                                        waits=[(PE, last), v_b1],
                                        incs=[(ACT, 1)])
                                    h_evac[fs] = h_slot_free[pslot]
                        if nxt < NE * NFC:
                            dma_w1(en, fcn, 0)
                            dma_w1(en, fcn, 1)
                        for tc in range(8):
                            ypar = tc % 2
                            last = 0
                            for fs in range(NFSUB):
                                for ch in range(2):
                                    w = []
                                    if fs == 0 and ch == 0:
                                        w = [w2_dma[(e, fc)],
                                             (ACT, h_evac[NFSUB - 1]),
                                             (DVE, max(y_par_free.get(ypar, 0), dve_snap))]
                                    last = bld.emit(
                                        "tensor",
                                        lambda t, tc=tc, fs=fs, ch=ch, ypar=ypar, w2slot=w2slot: t.matmul(
                                            psB[:, ypar * 1024 + ch * 512: ypar * 1024 + (ch + 1) * 512],
                                            h_sb[:, fs * 1024 + tc * 128: fs * 1024 + (tc + 1) * 128],
                                            w2c[:, w2slot * 8192 + fs * 1024 + ch * 512: w2slot * 8192 + fs * 1024 + (ch + 1) * 512],
                                            start=(fs == 0), stop=(fs == NFSUB - 1)),
                                        waits=w,
                                        incs=([(PE, 1)] if (fs == NFSUB - 1 and ch == 1) else []))
                            w2_free[w2slot] = last
                            y_par_free[ypar] = bld.emit(
                                "vector",
                                lambda v, tc=tc, ypar=ypar, e=e, span=span: v.scalar_tensor_tensor(
                                    y_acc[:, tc * 1024:(tc + 1) * 1024],
                                    psB[:, ypar * 1024:(ypar + 1) * 1024],
                                    wt_sp[:, span * 64 + tc * 8 + e: span * 64 + tc * 8 + e + 1],
                                    y_acc[:, tc * 1024:(tc + 1) * 1024],
                                    ALU.mult, ALU.add),
                                waits=[(PE, last), (ACT, b2init_done[tc]), (DVE, wt_done[tc])],
                                incs=[(DVE, 1)])
                            stt_done[tc] = y_par_free[ypar]
                if DEBUG:
                    SQ.dma(lambda s, seq=seq, span=span: s.dma_start(dbg_wt[seq, span], wt_sp[:, span * 64:(span + 1) * 64]),
                           waits=[(DVE, max(wt_done))])
                    SQ.dma(lambda s, seq=seq, span=span: s.dma_start(dbg_y[seq, span], y_acc),
                           waits=[(DVE, max(stt_done.values()))])
                for tc in range(8):
                    trk[("g", seq, span, tc)] = bld.emit(
                        "scalar",
                        lambda s, tc=tc, span=span: s.activation(
                            g_park[:, (span * 8 + tc) * 1024: (span * 8 + tc + 1) * 1024],
                            y_acc[:, tc * 1024:(tc + 1) * 1024], AF.Silu),
                        waits=[(DVE, stt_done[tc])], incs=[(ACT, 1)])
            trk[("g_done", seq)] = ACT.n
            if DEBUG:
                SQ.dma(lambda s, seq=seq: s.dma_start(dbg_g[seq], g_park[:, :]),
                       waits=[(ACT, trk[("g_done", seq)])])

        def phase_attn(seq):
            xn_ok = trk[("xn_done", seq)]
            v_vt = GQ.dma(
                lambda g: g.dma_start(vt_ot.rearrange("p (cc h) -> p cc h", h=1024), vt_h.rearrange("(cc p) h -> p cc h", p=128)))
            ones_ap = v_aug.rearrange("p (a b) -> p a b", b=65)[:, :, 64:65]
            v_ones = bld.emit("vector", lambda v: v.memset(ones_ap, 1.0), incs=[(DVE, 1)])
            vevac = {}
            vproj_last = 0
            for tkc in range(16):
                par = tkc % 2
                last = 0
                for cc in range(8):
                    for ch in range(2):
                        w = []
                        if cc == 0 and ch == 0:
                            w = [v_vt, (DVE, xn_ok), (ACT, vevac.get(par, 0))]
                        last = bld.emit(
                            "tensor",
                            lambda t, tkc=tkc, cc=cc, ch=ch, par=par: t.matmul(
                                psA[:, par * 1024 + ch * 512: par * 1024 + (ch + 1) * 512],
                                xn_fm[:, cc * 2048 + tkc * 128: cc * 2048 + (tkc + 1) * 128],
                                vt_ot[:, cc * 1024 + ch * 512: cc * 1024 + (ch + 1) * 512],
                                start=(cc == 0), stop=(cc == 7)),
                            waits=w,
                            incs=([(PE, 1)] if (cc == 7 and ch == 1) else []))
                vproj_last = last
                ev = 0
                for h in range(16):
                    ev = bld.emit(
                        "scalar",
                        lambda s, tkc=tkc, h=h, par=par: s.activation(
                            v_aug[:, tkc * 1040 + h * 65: tkc * 1040 + h * 65 + 64],
                            psA[:, par * 1024 + h * 64: par * 1024 + (h + 1) * 64], AF.Copy),
                        waits=([(PE, last), (DVE, v_ones)] if h == 0 else []),
                        incs=([(ACT, 1)] if h == 15 else []))
                vevac[par] = ev
            v_all_evac = max(vevac.values())
            qk_dma = {}
            qk_free = {}
            qk_evac = {}

            def dma_qk(hp):
                for i, wh in enumerate((qt_h, kt_h)):
                    slot = (hp % 2) * 2 + i
                    src = wh[:, hp * 128:(hp + 1) * 128].rearrange("(cc p) h -> p cc h", p=128)
                    dst = qk_w[:, slot * 1024:(slot + 1) * 1024].rearrange("p (cc h) -> p cc h", h=128)
                    qk_dma[(hp, i)] = GQ.dma(
                        lambda g, d=dst, s=src: g.dma_start(d, s),
                        waits=[(PE, qk_free.get(slot, vproj_last))])

            last_scores = [0]
            lane_exp = [0, 0, 0, 0]
            lane_avh = [[0, 0] for _ in range(4)]
            lane_step = [0, 0, 0, 0]
            lane_zevac = [(0, 0, None) for _ in range(4)]
            lane_sumtok = [None, None, None, None]
            dma_qk(0)
            for hp in range(8):
                if hp + 1 < 8:
                    dma_qk(hp + 1)
                for i, dstbuf in enumerate((q_hp, k_hp)):
                    slot = (hp % 2) * 2 + i
                    last = 0
                    for cc in range(8):
                        for tt in range(4):
                            w = []
                            if cc == 0 and tt == 0:
                                prev = (hp, 0) if i == 1 else (hp - 1, 1)
                                w = [qk_dma[(hp, i)], (DVE, xn_ok),
                                     (ACT, max(max(lane_exp), v_all_evac)),
                                     (DVE, qk_evac.get(prev, 0))]
                            last = bld.emit(
                                "tensor",
                                lambda t, slot=slot, cc=cc, tt=tt: t.matmul(
                                    psA[:, tt * 512:(tt + 1) * 512],
                                    qk_w[:, slot * 1024 + cc * 128: slot * 1024 + (cc + 1) * 128],
                                    xn_fm[:, cc * 2048 + tt * 512: cc * 2048 + (tt + 1) * 512],
                                    start=(cc == 0), stop=(cc == 7)),
                                waits=w,
                                incs=([(PE, 1)] if (cc == 7 and tt == 3) else []))
                    qk_free[slot] = last
                    qk_evac[(hp, i)] = bld.emit(
                        "vector",
                        lambda v, dstbuf=dstbuf: v.tensor_copy(dstbuf[:, :], psA[:, :]),
                        waits=[(PE, last)], incs=[(DVE, 1)])
                lanes_units = [[(0, 3), (0, 0)], [(1, 3), (1, 0)], [(0, 2), (0, 1)], [(1, 2), (1, 1)]]
                lstate = []
                for l in range(4):
                    lstate.append({"q": list(lanes_units[l]), "u": None, "t": 0, "pend": None})

                def emit_scores(l):
                    st_ = lstate[l]
                    h2u, vq = st_["u"]
                    h = hp * 2 + h2u
                    hbu = h2u * 64
                    tkc = st_["t"]
                    ntile = 4 * vq + 4
                    ga0 = 512 * vq
                    tk0 = tkc * 128
                    a = max(tk0, ga0)
                    b_ = ga0 + 512
                    la = a - ga0
                    lb = b_ - ga0
                    par = lane_step[l] % 2
                    w = [(ACT, lane_exp[l])]
                    if tkc == 0:
                        w.append((DVE, qk_evac[(hp, 1)]))
                    sc_v = bld.emit(
                        "tensor",
                        lambda t, l=l, la=la, lb=lb, a=a, b_=b_, tk0=tk0, hbu=hbu: t.matmul(
                            psA[:, l * 512 + la: l * 512 + lb],
                            k_hp[hbu:hbu + 64, tk0:tk0 + 128],
                            q_hp[hbu:hbu + 64, a:b_],
                            start=True, stop=True),
                        waits=w, incs=[(PE, 1)])
                    last_scores[0] = sc_v
                    prev_av = lane_avh[l][par]
                    lane_exp[l] = bld.emit(
                        "scalar",
                        lambda s, l=l, par=par, la=la, lb=lb: s.activation(
                            attnT[:, l * 1024 + par * 512 + la: l * 1024 + par * 512 + lb],
                            psA[:, l * 512 + la: l * 512 + lb], AF.Exp, bias=shift_sb[:, 0:1]),
                        waits=[(PE, sc_v), (PE, prev_av), consts_done_X],
                        incs=[(ACT, 1)])
                    mval = 0
                    if tk0 >= ga0:
                        mval = bld.emit(
                            "vector",
                            lambda v, l=l, par=par, la=la: v.tensor_tensor(
                                attnT[:, l * 1024 + par * 512 + la: l * 1024 + par * 512 + la + 128],
                                attnT[:, l * 1024 + par * 512 + la: l * 1024 + par * 512 + la + 128],
                                mask_bf[:, :], ALU.mult),
                            waits=[(ACT, lane_exp[l]), v_mask], incs=[(DVE, 1)])
                    st_["pend"] = (h2u, vq, tkc, par, la, lb, lane_exp[l], mval)
                    lane_step[l] += 1
                    st_["t"] += 1

                def emit_av(l):
                    st_ = lstate[l]
                    h2u, vq, tkc, par, la, lb, expv, mval = st_["pend"]
                    st_["pend"] = None
                    h = hp * 2 + h2u
                    ntile = 4 * vq + 4
                    w = [(ACT, expv)]
                    if mval:
                        w.append((DVE, mval))
                    if tkc == 0:
                        za, zd, zt = lane_zevac[l]
                        w += [(ACT, max(za, v_all_evac)), (DVE, zd)]
                        if zt is not None:
                            w.append(zt)
                    av_v = bld.emit(
                        "tensor",
                        lambda t, l=l, par=par, la=la, lb=lb, tkc=tkc, h=h, stop_b=(tkc == ntile - 1): t.matmul(
                            psB[0:65, l * 512 + la: l * 512 + lb],
                            v_aug[:, tkc * 1040 + h * 65: tkc * 1040 + h * 65 + 65],
                            attnT[:, l * 1024 + par * 512 + la: l * 1024 + par * 512 + lb],
                            start=(tkc == 0), stop=stop_b),
                        waits=w,
                        incs=[(PE, 1)])
                    lane_avh[l][par] = av_v
                    if tkc == ntile - 1:
                        _, _, prev_tok = lane_zevac[l]
                        if h2u == 0:
                            za = bld.emit(
                                "scalar",
                                lambda s, l=l, hp=hp, vq=vq: s.activation(
                                    z_park[0:64, hp * 2048 + 512 * vq: hp * 2048 + 512 * vq + 512],
                                    psB[0:64, l * 512:(l + 1) * 512], AF.Copy),
                                waits=[(PE, av_v)], incs=[(ACT, 1)])
                            ztok = None
                        else:
                            zw = [(PE, av_v)]
                            if prev_tok is not None:
                                zw.append(prev_tok)
                            za = bld.emit(
                                "scalar",
                                lambda s, l=l: s.activation(
                                    zstage[0:64, l * 512:(l + 1) * 512],
                                    psB[0:64, l * 512:(l + 1) * 512], AF.Copy),
                                waits=zw, incs=[(ACT, 1)])
                            ztok = SQ.dma(
                                lambda s, l=l, hp=hp, vq=vq: s.dma_start(
                                    z_park[64:128, hp * 2048 + 512 * vq: hp * 2048 + 512 * vq + 512],
                                    zstage[0:64, l * 512:(l + 1) * 512]),
                                waits=[(ACT, za)])
                        sw = [(PE, av_v), (ACT, za)]
                        if lane_sumtok[l] is not None:
                            sw.append(lane_sumtok[l])
                        zd = bld.emit(
                            "vector",
                            lambda v, l=l: v.tensor_copy(
                                srstage[64:65, l * 512:(l + 1) * 512],
                                psB[64:65, l * 512:(l + 1) * 512]),
                            waits=sw, incs=[(DVE, 1)])
                        lane_sumtok[l] = SQ.dma(
                            lambda s, l=l, h=h, vq=vq: s.dma_start(
                                sums[h:h + 1, 512 * vq: 512 * vq + 512],
                                srstage[64:65, l * 512:(l + 1) * 512]),
                            waits=[(DVE, zd)])
                        lane_zevac[l] = (za, zd, ztok)
                        st_["u"] = None

                active = True
                while active:
                    active = False
                    for l in range(4):
                        st_ = lstate[l]
                        if st_["pend"] is not None:
                            emit_av(l)
                            active = True
                        if st_["u"] is None:
                            if st_["q"]:
                                st_["u"] = st_["q"].pop(0)
                                st_["t"] = 0
                            else:
                                continue
                        if st_["t"] < 4 * st_["u"][1] + 4:
                            emit_scores(l)
                            active = True
            v_ot = GQ.dma(
                lambda g: g.dma_start(vt_ot.rearrange("p (cc h) -> p cc h", h=1024), ot_h.rearrange("(cc p) h -> p cc h", p=128)),
                waits=[(PE, last_scores[0])])
            rec = bld.emit(
                "vector",
                lambda v: v.reciprocal(srstage[0:16, :], sums[0:16, :]),
                waits=SQ.all_done(),
                incs=[(DVE, 1)])
            vdrain()
            recb = bld.emit(
                "vector", lambda v: v.tensor_copy(recip_bf[:, :], srstage[0:16, :]),
                waits=[(DVE, rec)], incs=[(DVE, 1)])
            rep_free = {0: 0, 1: 0}
            for hp in range(8):
                rp = psA if hp % 2 == 0 else psB
                last = 0
                for c0 in range(0, T, 512):
                    w = []
                    if c0 == 0:
                        w = [(DVE, recb), v_sel, (DVE, rep_free[hp % 2])]
                        if hp < 2:
                            for (za, zd, zt) in lane_zevac:
                                w.append((ACT, za))
                                if zt is not None:
                                    w.append(zt)
                    last = bld.emit(
                        "tensor",
                        lambda t, rp=rp, c0=c0, hp=hp: t.matmul(
                            rp[:, c0:c0 + 512],
                            sel2_bf[0:16, hp * 128:(hp + 1) * 128],
                            recip_bf[0:16, c0:c0 + 512],
                            start=True, stop=True),
                        waits=w,
                        incs=([(PE, 1)] if c0 == 1536 else []))
                rep_free[hp % 2] = bld.emit(
                    "vector",
                    lambda v, rp=rp, hp=hp: v.tensor_tensor(
                        z_park[:, hp * 2048:(hp + 1) * 2048],
                        z_park[:, hp * 2048:(hp + 1) * 2048],
                        rp[:, :], ALU.mult),
                    waits=[(PE, last)], incs=[(DVE, 1)])
            znorm_all = DVE.n
            if DEBUG:
                SQ.dma(lambda s, seq=seq: s.dma_start(dbg_z[seq], z_park[:, :]),
                       waits=[(DVE, znorm_all)])
            xdma = {}
            combine = {}
            odma = {}

            def dma_xa(tqc):
                par = tqc % 2
                w = [(PE, qk_free.get(2, 0)), (PE, qk_free.get(3, 0))]
                if tqc >= 2:
                    w.append((DVE, combine[tqc - 2]))
                xdma[tqc] = SQ.dma(
                    lambda s, tqc=tqc, par=par: s.dma_start(
                        x_att[:, par * 1024:(par + 1) * 1024],
                        x_hbm[seq, tqc * 128:(tqc + 1) * 128, :]),
                    waits=w)

            dma_xa(0)
            dma_xa(1)
            g_ok = trk[("g_done", seq)]
            for tqc in range(16):
                par = tqc % 2
                sp = psB[:, par * 1024:(par + 1) * 1024]
                last = 0
                for ac in range(8):
                    for ch in range(2):
                        w = []
                        if ac == 0 and ch == 0:
                            w = [(DVE, znorm_all), v_ot, (DVE, combine.get(tqc - 2, 0))]
                        last = bld.emit(
                            "tensor",
                            lambda t, sp=sp, ac=ac, ch=ch, tqc=tqc: t.matmul(
                                sp[:, ch * 512:(ch + 1) * 512],
                                z_park[:, ac * 2048 + tqc * 128: ac * 2048 + (tqc + 1) * 128],
                                vt_ot[:, ac * 1024 + ch * 512: ac * 1024 + (ch + 1) * 512],
                                start=(ac == 0), stop=(ac == 7)),
                            waits=w,
                            incs=([(PE, 1)] if (ac == 7 and ch == 1) else []))
                bld.emit(
                    "vector",
                    lambda v, sp=sp, tqc=tqc, par=par: v.tensor_tensor(
                        out_st[:, par * 1024:(par + 1) * 1024], sp,
                        g_park[:, tqc * 1024:(tqc + 1) * 1024], ALU.mult),
                    waits=[(PE, last), (ACT, g_ok)] + ([odma[tqc - 2]] if (tqc - 2) in odma else []))
                vdrain()
                combine[tqc] = bld.emit(
                    "vector",
                    lambda v, tqc=tqc, par=par: v.tensor_tensor(
                        out_st[:, par * 1024:(par + 1) * 1024],
                        out_st[:, par * 1024:(par + 1) * 1024],
                        x_att[:, par * 1024:(par + 1) * 1024], ALU.add),
                    waits=[xdma[tqc]], incs=[(DVE, 1)])
                odma[tqc] = SQ.dma(
                    lambda s, tqc=tqc, par=par: s.dma_start(
                        y_hbm[seq, tqc * 128:(tqc + 1) * 128, :],
                        out_st[:, par * 1024:(par + 1) * 1024]),
                    waits=[(DVE, combine[tqc])])
                if tqc + 2 < 16:
                    dma_xa(tqc + 2)

        import os
        PH = os.environ.get("K_PHASES", "lfa")
        for seq in range(NSEQ):
            bld.barrier()
            phase_ln(seq)
            if "f" in PH:
                phase_ffn(seq)
            if "a" in PH:
                bld.barrier()
                phase_attn(seq)
        bld.emit("sync", lambda s: None, waits=SQ.all_done())

        @block.tensor
        def _(t):
            for fn in bld.q["tensor"]:
                fn(t)

        @block.scalar
        def _(s):
            for fn in bld.q["scalar"]:
                fn(s)

        @block.vector
        def _(v):
            for fn in bld.q["vector"]:
                fn(v)

        @block.sync
        def _(s):
            for fn in bld.q["sync"]:
                fn(s)

        @block.gpsimd
        def _(g):
            for fn in bld.q["gpsimd"]:
                fn(g)

    return nc


def make_in_maps(inputs):
    x = np.ascontiguousarray(np.asarray(inputs["L0_attn_x"], dtype=np.float32))
    Q = np.asarray(inputs["L0_attn_Q"], dtype=np.float32)
    K = np.asarray(inputs["L0_attn_K"], dtype=np.float32)
    V = np.asarray(inputs["L0_attn_V"], dtype=np.float32)
    Om = np.asarray(inputs["L0_attn_O"], dtype=np.float32)
    rw = np.asarray(inputs["L0_ffn_router_w"], dtype=np.float32)
    w1 = np.asarray(inputs["L0_ffn_w1"], dtype=np.float32)
    b1 = np.asarray(inputs["L0_ffn_b1"], dtype=np.float32)
    w2 = np.asarray(inputs["L0_ffn_w2"], dtype=np.float32)
    b2 = np.asarray(inputs["L0_ffn_b2"], dtype=np.float32)

    qt = np.ascontiguousarray(Q.reshape(1024, 1024).T)
    kt = np.ascontiguousarray(K.reshape(1024, 1024).T)
    vt = np.ascontiguousarray(V.reshape(1024, 1024).T)
    ot = np.ascontiguousarray(Om.T)
    rwt = np.ascontiguousarray(rw.T)
    w1t = np.ascontiguousarray(w1.transpose(0, 2, 1))
    w2t = np.ascontiguousarray(w2.transpose(0, 2, 1))
    b1r = np.ascontiguousarray(b1.reshape(8, 32, 128).transpose(2, 0, 1))
    ident = np.eye(128, dtype=np.float32)
    mask = np.triu(np.ones((128, 128), dtype=np.float32))
    sel2 = np.zeros((16, 1024), dtype=np.float32)
    for hp in range(8):
        sel2[2 * hp, hp * 128: hp * 128 + 64] = 1.0
        sel2[2 * hp + 1, hp * 128 + 64: (hp + 1) * 128] = 1.0

    shiftc = np.full((128, 1), -SHIFT, dtype=np.float32)
    shared = dict(qt=qt, kt=kt, vt=vt, ot=ot, rwt=rwt, w1t=w1t, w2t=w2t,
                  b1r=b1r, b2=np.ascontiguousarray(b2), ident=ident, mask=mask, sel2=sel2,
                  shiftc=shiftc)
    in_maps = []
    for c in range(8):
        m = dict(shared)
        m["x"] = np.ascontiguousarray(x[2 * c: 2 * c + 2])
        in_maps.append(m)
    return in_maps


def kernel(**inputs):
    nc = build_nc()
    in_maps = make_in_maps(inputs)
    res = run_bass_kernel_spmd(nc, in_maps, core_ids=list(range(8)))
    out = np.empty((16, T, D), np.float32)
    for c in range(8):
        out[2 * c: 2 * c + 2] = np.asarray(res.results[c]["y"])
    return out


# version 08
# speedup vs baseline: 1.1412x; 1.1412x over previous
import numpy as np
import concourse.bass as bass
import concourse.mybir as mybir
from concourse.bass_utils import run_bass_kernel_spmd

F32 = mybir.dt.float32
BF16 = mybir.dt.bfloat16
AF = mybir.ActivationFunctionType
ALU = mybir.AluOpType
AX = mybir.AxisListType

D = 1024
T = 2048
NSEQ = 2
NE = 8
FF = 4096
SPAN = 1024
NSPAN = T // SPAN
FC = 1024
NFC = FF // FC
NFSUB = FC // 128
SHIFT = 30.0
EPS = 1e-5

DEBUG = False

ENGS = ("tensor", "scalar", "vector", "sync", "gpsimd")


class Sem:
    __slots__ = ("h", "name", "n")

    def __init__(self, handle, name):
        self.h = handle
        self.name = name
        self.n = 0


class B:

    def __init__(self, sems):
        self.q = {e: [] for e in ENGS}
        self.sems = sems
        self.pending = {e: None for e in ENGS}

    def barrier(self):
        snap = [(s, s.n) for s in self.sems]
        for e in ENGS:
            self.pending[e] = list(snap)

    def emit(self, eng, fn, waits=(), incs=()):
        for (s, v) in waits:
            assert v <= s.n, f"wait-on-future {s.name}: {v} > {s.n}"
        bar = self.pending[eng]
        self.pending[eng] = None
        out_vals = []
        for (s, step) in incs:
            s.n += step
            out_vals.append(s.n)
        waits = tuple((s, v) for (s, v) in waits if v > 0)
        incs = tuple(incs)

        def run(e, bar=bar, waits=waits, incs=incs, fn=fn):
            if bar is not None:
                for (s, v) in bar:
                    if v > 0:
                        e.wait_ge(s.h, v)
            for (s, v) in waits:
                e.wait_ge(s.h, v)
            ins = fn(e)
            for (s, step) in incs:
                ins.then_inc(s.h, step)

        self.q[eng].append(run)
        if len(out_vals) == 1:
            return out_vals[0]
        return out_vals


class DmaQ:

    def __init__(self, bld, eng, sems):
        self.bld = bld
        self.eng = eng
        self.sems = sems
        self.i = 0

    def dma(self, fn, waits=()):
        s = self.sems[self.i % len(self.sems)]
        self.i += 1
        w = list(waits) + [(s, s.n)]
        v = self.bld.emit(self.eng, fn, waits=w, incs=[(s, 16)])
        return (s, v)

    def all_done(self):
        return [(s, s.n) for s in self.sems]


def build_nc():
    nc = bass.Bass("TRN2", target_bir_lowering=False)

    x_hbm = nc.dram_tensor("x", [NSEQ, T, D], F32, kind="ExternalInput").ap()
    qt_h = nc.dram_tensor("qt", [D, D], F32, kind="ExternalInput").ap()
    kt_h = nc.dram_tensor("kt", [D, D], F32, kind="ExternalInput").ap()
    vt_h = nc.dram_tensor("vt", [D, D], F32, kind="ExternalInput").ap()
    ot_h = nc.dram_tensor("ot", [D, D], F32, kind="ExternalInput").ap()
    rwt_h = nc.dram_tensor("rwt", [D, NE], F32, kind="ExternalInput").ap()
    w1t_h = nc.dram_tensor("w1t", [NE, D, FF], F32, kind="ExternalInput").ap()
    w2t_h = nc.dram_tensor("w2t", [NE, FF, D], F32, kind="ExternalInput").ap()
    b1r_h = nc.dram_tensor("b1r", [128, NE, 32], F32, kind="ExternalInput").ap()
    b2_h = nc.dram_tensor("b2", [NE, D], F32, kind="ExternalInput").ap()
    id_h = nc.dram_tensor("ident", [128, 128], F32, kind="ExternalInput").ap()
    mk_h = nc.dram_tensor("mask", [128, 128], F32, kind="ExternalInput").ap()
    sel_h = nc.dram_tensor("sel2", [16, 1024], F32, kind="ExternalInput").ap()
    shf_h = nc.dram_tensor("shiftc", [128, 2], F32, kind="ExternalInput").ap()
    y_hbm = nc.dram_tensor("y", [NSEQ, T, D], F32, kind="ExternalOutput").ap()
    if DEBUG:
        dbg_xn = nc.dram_tensor("dbg_xn", [NSEQ, 128, 16384], BF16, kind="ExternalOutput").ap()
        dbg_g = nc.dram_tensor("dbg_g", [NSEQ, 128, 16384], BF16, kind="ExternalOutput").ap()
        dbg_z = nc.dram_tensor("dbg_z", [NSEQ, 128, 16384], BF16, kind="ExternalOutput").ap()
        dbg_wt = nc.dram_tensor("dbg_wt", [NSEQ, NSPAN, 128, 64], F32, kind="ExternalOutput").ap()
        dbg_y = nc.dram_tensor("dbg_y", [NSEQ, NSPAN, 128, 8192], F32, kind="ExternalOutput").ap()

    from contextlib import ExitStack
    es = ExitStack()
    sb = lambda name, shape, dt: es.enter_context(nc.sbuf_tensor(name, shape, dt))
    ps = lambda name, shape, dt: es.enter_context(nc.psum_tensor(name, shape, dt))
    sem = lambda name: es.enter_context(nc.semaphore(name))
    with es:
        xn_fm = sb("xn_fm", [128, 16384], BF16)
        g_park = sb("g_park", [128, 16384], BF16)
        arena_b = sb("arena_b", [128, 50432], BF16)
        arena_f = sb("arena_f", [128, 8192], F32)
        ident_f = sb("ident_f", [128, 128], F32)
        mask_bf = sb("mask_bf", [128, 128], BF16)
        rwt_bf = sb("rwt_bf", [128, 64], BF16)
        b1_sb = sb("b1_sb", [128, 256], F32)
        b2_bf = sb("b2_bf", [8, 1024], BF16)
        wt_sp = sb("wt_sp", [128, 128], F32)
        wtt_sb = sb("wtt_sb", [8, 2048], BF16)
        rt_lg = sb("rt_lg", [128, 8], F32)
        rt_rank = sb("rt_rank", [128, 8], F32)
        rt_cmp = sb("rt_cmp", [128, 8], F32)
        rt_E = sb("rt_E", [128, 8], F32)
        rt_Em = sb("rt_Em", [128, 8], F32)
        rt_S = sb("rt_S", [128, 1], F32)
        rt_d = sb("rt_d", [128, 1], F32)
        rt_rcp = sb("rt_rcp", [128, 1], F32)
        st_sum = sb("st_sum", [128, 2], F32)
        st_ssq = sb("st_ssq", [128, 2], F32)
        st_ngm = sb("st_ngm", [128, 2], F32)
        st_var = sb("st_var", [128, 2], F32)
        st_tmp = sb("st_tmp", [128, 2], F32)
        st_rstd = sb("st_rstd", [128, 2], F32)
        shift_sb = sb("shift_sb", [128, 2], F32)
        psA = ps("psA", [128, 2048], F32)
        psB = ps("psB", [128, 2048], F32)
        _sg = [sem(f"sG{i}") for i in range(4)]
        _ss = [sem(f"sS{i}") for i in range(4)]
        _sPE = sem("sPE"); _sACT = sem("sACT"); _sDVE = sem("sDVE")
        block = es.enter_context(nc.Block())

        GQS = [Sem(h, f"G{i}") for i, h in enumerate(_sg)]
        SQS = [Sem(h, f"S{i}") for i, h in enumerate(_ss)]
        PE = Sem(_sPE, "PE")
        ACT = Sem(_sACT, "ACT")
        DVE = Sem(_sDVE, "DVE")
        bld = B(GQS + SQS + [PE, ACT, DVE])
        GQ = DmaQ(bld, "gpsimd", GQS)
        SQ = DmaQ(bld, "sync", SQS)

        def cb(col_off, parts, cols):
            return arena_b[0:parts, col_off: col_off + cols]

        def cf(col_off, parts, cols):
            return arena_f[0:parts, col_off: col_off + cols]

        y_acc = cf(0, 128, 8192)
        h_sb = cb(0, 128, 8192)
        w1r = cb(8192, 128, 8192)
        w2c = cb(16384, 128, 16384)
        x_ln = cf(0, 128, 2048)
        xn_tmp = cf(2048, 128, 2048)
        ln_sq = cf(4096, 128, 2048)
        v_aug = cb(0, 128, 16640)
        z_park = cb(16640, 128, 16384)
        vt_ot = cb(33024, 128, 8192)
        qk_w = cb(33024, 128, 4096)
        q_hp = cb(37120, 128, 2048)
        k_hp = cb(39168, 128, 2048)
        attnT = cb(41216, 128, 4096)
        recip_bf = cb(45312, 16, 2048)
        sel2_bf = cb(47360, 16, 1024)
        zstage = cb(48384, 64, 2048)
        sums = cf(0, 16, 2048)
        srstage = cf(2048, 128, 2048)
        x_att = cf(4096, 128, 2048)
        out_st = cf(6144, 128, 2048)

        v_ident = SQ.dma(lambda s: s.dma_start(ident_f[:, :], id_h[:, :]))
        v_mask = GQ.dma(lambda g: g.dma_start(mask_bf[:, :], mk_h[:, :]))
        v_sel = GQ.dma(lambda g: g.dma_start(sel2_bf[:, :], sel_h[:, :]))
        v_rwt = GQ.dma(
            lambda g: g.dma_start(rwt_bf.rearrange("p (cc e) -> p cc e", e=8), rwt_h.rearrange("(cc p) e -> p cc e", p=128)))
        v_b2 = GQ.dma(lambda g: g.dma_start(b2_bf[:, :], b2_h[:, :]))
        v_b1 = SQ.dma(
            lambda s: s.dma_start(b1_sb[:, :], b1r_h.rearrange("p e f -> p (e f)")))
        v_shf = SQ.dma(lambda s: s.dma_start(shift_sb[:, :], shf_h[:, :]))
        consts_done_X = v_shf

        def vdrain():
            bld.emit("vector", lambda v: v.drain())

        trk = {}

        def phase_ln(seq):
            xdma = [0] * 16
            trevac = {}
            xnw = [0] * 16
            lastT = [0] * 16
            rstdv = [0] * 16

            def dma_x(tc):
                par = tc % 2
                w = []
                if tc >= 2:
                    w = [(DVE, xnw[tc - 2])]

                src = x_hbm[seq, tc * 128:(tc + 1) * 128, :]
                dst = x_ln[:, par * 1024:(par + 1) * 1024]
                xdma[tc] = SQ.dma(lambda s, d=dst, sr=src: s.dma_start(d, sr), waits=w)

            dma_x(0)
            dma_x(1)
            statv = [0] * 16

            def emit_stats(tc):
                p = tc % 2
                xt = x_ln[:, (tc % 2) * 1024:((tc % 2) + 1) * 1024]
                bld.emit("vector",
                         lambda v, xt=xt, p=p: v.reduce_sum(st_sum[:, p:p + 1], xt, axis=AX.X),
                         waits=[xdma[tc], (ACT, rstdv[tc - 2] if tc >= 2 else 0)])
                bld.emit("vector", lambda v, xt=xt, p=p: v.tensor_tensor(ln_sq[:, p * 1024:(p + 1) * 1024], xt, xt, ALU.mult))
                vdrain()
                bld.emit("vector", lambda v, p=p: v.reduce_sum(st_ssq[:, p:p + 1], ln_sq[:, p * 1024:(p + 1) * 1024], axis=AX.X))
                bld.emit("vector", lambda v, p=p: v.tensor_scalar(st_ngm[:, p:p + 1], st_sum[:, p:p + 1], -1.0 / D, None, ALU.mult))
                vdrain()
                bld.emit("vector", lambda v, p=p: v.tensor_scalar(st_tmp[:, p:p + 1], st_ssq[:, p:p + 1], 1.0 / D, None, ALU.mult))
                vdrain()
                bld.emit("vector", lambda v, p=p: v.tensor_tensor(st_var[:, p:p + 1], st_ngm[:, p:p + 1], st_ngm[:, p:p + 1], ALU.mult))
                vdrain()
                bld.emit("vector", lambda v, p=p: v.tensor_sub(st_var[:, p:p + 1], st_tmp[:, p:p + 1], st_var[:, p:p + 1]))
                vdrain()
                statv[tc] = bld.emit("vector", lambda v, p=p: v.tensor_scalar(st_var[:, p:p + 1], st_var[:, p:p + 1], EPS, None, ALU.add),
                                     incs=[(DVE, 1)])
                lnv = bld.emit("scalar", lambda s, p=p: s.activation(st_tmp[:, p:p + 1], st_var[:, p:p + 1], AF.Ln),
                               waits=[(DVE, statv[tc])], incs=[(ACT, 1)])
                rstdv[tc] = bld.emit(
                    "scalar", lambda s, p=p: s.activation(st_rstd[:, p:p + 1], st_tmp[:, p:p + 1], AF.Exp, scale=shift_sb[:, 1:2]),
                    waits=[(ACT, lnv), (DVE, xnw[tc - 2] if tc >= 2 else 0)], incs=[(ACT, 1)])

            emit_stats(0)
            for tc in range(16):
                par = tc % 2
                xt = x_ln[:, par * 1024:(par + 1) * 1024]
                xnt = xn_tmp[:, par * 1024:(par + 1) * 1024]
                if tc + 1 < 16:
                    emit_stats(tc + 1)
                xnw[tc] = bld.emit(
                    "vector",
                    lambda v, xt=xt, xnt=xnt, par=par: v.tensor_scalar(xnt, xt, st_ngm[:, par:par + 1], st_rstd[:, par:par + 1], ALU.add, ALU.mult),
                    waits=[(ACT, rstdv[tc]), (PE, lastT[tc - 2] if tc >= 2 else 0)],
                    incs=[(DVE, 1)])
                for cc in range(8):
                    bank = (tc * 8 + cc) % 4
                    tdst = psA[:, bank * 512: bank * 512 + 128]
                    tsrc = xnt[:, cc * 128:(cc + 1) * 128]
                    lastT[tc] = bld.emit(
                        "tensor",
                        lambda t, d=tdst, s=tsrc: t.transpose(d, s, ident_f[:, :]),
                        waits=[(DVE, max(xnw[tc], trevac.get(bank, 0))), v_ident],
                        incs=[(PE, 1)])
                    edst = xn_fm[:, cc * 2048 + tc * 128: cc * 2048 + (tc + 1) * 128]
                    trevac[bank] = bld.emit(
                        "vector",
                        lambda v, d=edst, s=tdst: v.tensor_copy(d, s),
                        waits=[(PE, lastT[tc])],
                        incs=[(DVE, 1)])
                trk[("xn_tc", seq, tc)] = max(trevac.values())
                if tc + 2 < 16:
                    dma_x(tc + 2)
                if tc == 7:
                    emit_router(seq, 0)
                if tc == 15:
                    emit_router(seq, 1)
            trk[("xn_done", seq)] = max(trevac.values())
            if DEBUG:
                SQ.dma(lambda s, seq=seq: s.dma_start(dbg_xn[seq], xn_fm[:, :]),
                       waits=[(DVE, trk[("xn_done", seq)])])

        def emit_router(seq, span):
            sp0 = span * SPAN
            dveb_snap = max(DVE.n and trk.get(("router_psB_free", seq), 0), trk.get(("router_psB_free", seq), 0))
            lg_use = [0, 0]
            exp_done = [0, 0]
            wt_done = [0] * 8
            if True:
                for tc in range(8):
                    par = tc % 2
                    lg = psB[:, par * 512: par * 512 + 8]
                    last = 0
                    for cc in range(8):
                        w = []
                        if cc == 0:
                            w = [(DVE, trk[("xn_tc", seq, span * 8 + tc)]), v_rwt, (DVE, lg_use[par]), (DVE, dveb_snap)]
                        last = bld.emit(
                            "tensor",
                            lambda t, lg=lg, cc=cc, tc=tc, sp0=sp0: t.matmul(
                                lg,
                                xn_fm[:, cc * 2048 + sp0 + tc * 128: cc * 2048 + sp0 + (tc + 1) * 128],
                                rwt_bf[:, cc * 8:(cc + 1) * 8],
                                start=(cc == 0), stop=(cc == 7)),
                            waits=w,
                            incs=([(PE, 1)] if cc == 7 else []))
                    lg_use[par] = bld.emit(
                        "vector", lambda v, lg=lg: v.tensor_copy(rt_lg[:, :], lg),
                        waits=[(PE, last), (ACT, exp_done[0]), (ACT, exp_done[1])],
                        incs=[(DVE, 1)])
                    vdrain()
                    bld.emit("vector", lambda v: v.tensor_scalar(rt_rank[:, :], rt_lg[:, :], rt_lg[:, 0:1], None, ALU.is_lt))
                    for j in range(1, 8):
                        vdrain()
                        bld.emit("vector", lambda v, j=j: v.tensor_scalar(rt_cmp[:, :], rt_lg[:, :], rt_lg[:, j:j + 1], None, ALU.is_lt))
                        vdrain()
                        bld.emit("vector", lambda v: v.tensor_add(rt_rank[:, :], rt_rank[:, :], rt_cmp[:, :]))
                    for j in range(0, 7):
                        vdrain()
                        bld.emit("vector", lambda v, j=j: v.tensor_scalar(
                            rt_cmp[:, j + 1:8], rt_lg[:, j + 1:8], rt_lg[:, j:j + 1], None, ALU.is_equal))
                        vdrain()
                        bld.emit("vector", lambda v, j=j: v.tensor_add(rt_rank[:, j + 1:8], rt_rank[:, j + 1:8], rt_cmp[:, j + 1:8]))
                    vdrain()
                    bld.emit("vector", lambda v: v.tensor_scalar(rt_cmp[:, :], rt_rank[:, :], 1.5, None, ALU.is_le))
                    vdrain()
                    exp_done[par] = bld.emit(
                        "scalar", lambda s: s.activation(rt_E[:, :], rt_lg[:, :], AF.Exp),
                        waits=[(DVE, lg_use[par])], incs=[(ACT, 1)])
                    bld.emit("vector", lambda v: v.reduce_sum(rt_S[:, :], rt_E[:, :], axis=AX.X),
                             waits=[(ACT, exp_done[par])])
                    bld.emit("vector", lambda v: v.tensor_mul(rt_Em[:, :], rt_E[:, :], rt_cmp[:, :]))
                    vdrain()
                    bld.emit("vector", lambda v: v.reduce_sum(rt_d[:, :], rt_Em[:, :], axis=AX.X))
                    vdrain()
                    bld.emit("vector", lambda v: v.scalar_tensor_tensor(rt_d[:, :], rt_S[:, :], 1e-9, rt_d[:, :], ALU.mult, ALU.add))
                    vdrain()
                    bld.emit("vector", lambda v: v.reciprocal(rt_rcp[:, :], rt_d[:, :]))
                    vdrain()
                    wt_done[tc] = bld.emit(
                        "vector", lambda v, tc=tc, span=span: v.tensor_scalar(
                            wt_sp[:, span * 64 + tc * 8: span * 64 + (tc + 1) * 8], rt_Em[:, :], rt_rcp[:, 0:1], None, ALU.mult),
                        incs=[(DVE, 1)])

                lg_done = max(lg_use)
                wtt_done = [0] * 8
                for tc in range(8):
                    slot = tc % 2
                    tdst = psB[0:8, (2 + slot) * 512: (2 + slot) * 512 + 128]
                    w = [(DVE, wt_done[tc])]
                    if tc == 0:
                        w.append((DVE, dveb_snap))
                    if tc >= 2:
                        w.append((DVE, wtt_done[tc - 2]))
                    wtt_pe = bld.emit(
                        "tensor",
                        lambda t, d=tdst, tc=tc, span=span: t.transpose(
                            d, wt_sp[:, span * 64 + tc * 8: span * 64 + (tc + 1) * 8], ident_f[:, :]),
                        waits=w, incs=[(PE, 1)])
                    wtt_done[tc] = bld.emit(
                        "vector",
                        lambda v, d=tdst, tc=tc, span=span: v.tensor_copy(
                            wtt_sb[0:8, span * 1024 + tc * 128: span * 1024 + (tc + 1) * 128], d),
                        waits=[(PE, wtt_pe)], incs=[(DVE, 1)])
            trk[("router", seq, span)] = (wt_done, wtt_done, max(lg_use))
            trk[("router_psB_free", seq)] = DVE.n

        def phase_ffn(seq):
            w1_dma = trk.setdefault("w1_dma", {})
            w2_dma = trk.setdefault("w2_dma", {})
            w1_free = trk.setdefault("w1_free", {})
            w2_free = trk.setdefault("w2_free", {})
            h_slot_free = trk.setdefault("h_slot_free", {})
            y_par_free = trk.setdefault("y_par_free", {})
            router_dve = DVE.n

            for span in range(NSPAN):
                sp0 = span * SPAN
                dve_snap = DVE.n
                wt_done, wtt_done, lg_done = trk[("router", seq, span)]
                xn_ok = trk[("xn_tc", seq, span * 8 + 7)] if span == 0 else trk[("xn_tc", seq, 15)]
                wtt_last = max(wtt_done)
                b2init_done = [0] * 8
                for tc in range(8):
                    par = tc % 2
                    last = 0
                    for ch in range(2):
                        w = []
                        if ch == 0:
                            w = [(DVE, max(router_dve, dve_snap)), v_b2]
                            if tc >= 2:
                                w.append((ACT, b2init_done[tc - 2]))
                        last = bld.emit(
                            "tensor",
                            lambda t, tc=tc, ch=ch, par=par, span=span: t.matmul(
                                psB[:, par * 1024 + ch * 512: par * 1024 + (ch + 1) * 512],
                                wtt_sb[0:8, span * 1024 + tc * 128: span * 1024 + (tc + 1) * 128],
                                b2_bf[0:8, ch * 512:(ch + 1) * 512],
                                start=True, stop=True),
                            waits=w,
                            incs=([(PE, 1)] if ch == 1 else []))
                    b2init_done[tc] = bld.emit(
                        "scalar",
                        lambda s, tc=tc, par=par: s.activation(
                            y_acc[:, tc * 1024:(tc + 1) * 1024], psB[:, par * 1024:(par + 1) * 1024], AF.Copy),
                        waits=[(PE, last)], incs=[(ACT, 1)])

                h_evac = {}
                stt_done = {}

                def dma_w1(e, fc, ch2):
                    idx = (e * NFC + fc) * 2 + ch2
                    slot = idx % 2
                    f0 = fc * FC + ch2 * 512
                    src = w1t_h[e, :, f0:f0 + 512].rearrange("(cc p) f -> p cc f", p=128)
                    dst = w1r[:, slot * 4096:(slot + 1) * 4096].rearrange("p (cc f) -> p cc f", f=512)
                    w = [(PE, w1_free.get(slot, 0))]
                    w1_dma[(e, fc, ch2)] = GQ.dma(lambda g, d=dst, s=src: g.dma_start(d, s), waits=w)

                def dma_w2(e, fc):
                    idx = e * NFC + fc
                    slot = idx % 2
                    src = w2t_h[e, fc * FC:(fc + 1) * FC, :].rearrange("(fs p) c -> p fs c", p=128)
                    dst = w2c[:, slot * 8192:(slot + 1) * 8192].rearrange("p (fs c) -> p fs c", c=1024)
                    w = [(PE, w2_free.get(slot, 0))]
                    w2_dma[(e, fc)] = GQ.dma(lambda g, d=dst, s=src: g.dma_start(d, s), waits=w)

                dma_w1(0, 0, 0)
                dma_w1(0, 0, 1)
                dma_w2(0, 0)

                for e in range(NE):
                    for fc in range(NFC):
                        idx = e * NFC + fc
                        w2slot = idx % 2
                        nxt = idx + 1
                        if nxt < NE * NFC:
                            en, fcn = nxt // NFC, nxt % NFC
                            dma_w2(en, fcn)
                        for ch2 in range(2):
                            w1slot = ((e * NFC + fc) * 2 + ch2) % 2
                            for fl in range(4):
                                fs = ch2 * 4 + fl
                                for th in range(2):
                                    pslot = (fs * 2 + th) % 4
                                    pdst = psA[:, pslot * 512:(pslot + 1) * 512]
                                    last = 0
                                    for cc in range(8):
                                        w = []
                                        if cc == 0:
                                            w = [w1_dma[(e, fc, ch2)],
                                                 (ACT, h_slot_free.get(pslot, 0)),
                                                 (DVE, max(xn_ok, lg_done, wtt_last))]
                                        last = bld.emit(
                                            "tensor",
                                            lambda t, pdst=pdst, w1slot=w1slot, cc=cc, fl=fl, th=th, sp0=sp0: t.matmul(
                                                pdst,
                                                w1r[:, w1slot * 4096 + cc * 512 + fl * 128: w1slot * 4096 + cc * 512 + (fl + 1) * 128],
                                                xn_fm[:, cc * 2048 + sp0 + th * 512: cc * 2048 + sp0 + (th + 1) * 512],
                                                start=(cc == 0), stop=(cc == 7)),
                                            waits=w,
                                            incs=([(PE, 1)] if cc == 7 else []))
                                    w1_free[w1slot] = last
                                    bia = b1_sb[:, e * 32 + fc * 8 + fs: e * 32 + fc * 8 + fs + 1]
                                    h_slot_free[pslot] = bld.emit(
                                        "scalar",
                                        lambda s, fs=fs, th=th, pdst=pdst, bia=bia: s.activation(
                                            h_sb[:, fs * 1024 + th * 512: fs * 1024 + (th + 1) * 512],
                                            pdst, AF.Relu, bias=bia),
                                        waits=[(PE, last), v_b1],
                                        incs=[(ACT, 1)])
                                    h_evac[fs] = h_slot_free[pslot]
                        if nxt < NE * NFC:
                            dma_w1(en, fcn, 0)
                            dma_w1(en, fcn, 1)
                        for tc in range(8):
                            ypar = tc % 2
                            last = 0
                            for fs in range(NFSUB):
                                for ch in range(2):
                                    w = []
                                    if fs == 0 and ch == 0:
                                        w = [w2_dma[(e, fc)],
                                             (ACT, h_evac[NFSUB - 1]),
                                             (DVE, max(y_par_free.get(ypar, 0), dve_snap))]
                                    last = bld.emit(
                                        "tensor",
                                        lambda t, tc=tc, fs=fs, ch=ch, ypar=ypar, w2slot=w2slot: t.matmul(
                                            psB[:, ypar * 1024 + ch * 512: ypar * 1024 + (ch + 1) * 512],
                                            h_sb[:, fs * 1024 + tc * 128: fs * 1024 + (tc + 1) * 128],
                                            w2c[:, w2slot * 8192 + fs * 1024 + ch * 512: w2slot * 8192 + fs * 1024 + (ch + 1) * 512],
                                            start=(fs == 0), stop=(fs == NFSUB - 1)),
                                        waits=w,
                                        incs=([(PE, 1)] if (fs == NFSUB - 1 and ch == 1) else []))
                            w2_free[w2slot] = last
                            y_par_free[ypar] = bld.emit(
                                "vector",
                                lambda v, tc=tc, ypar=ypar, e=e, span=span: v.scalar_tensor_tensor(
                                    y_acc[:, tc * 1024:(tc + 1) * 1024],
                                    psB[:, ypar * 1024:(ypar + 1) * 1024],
                                    wt_sp[:, span * 64 + tc * 8 + e: span * 64 + tc * 8 + e + 1],
                                    y_acc[:, tc * 1024:(tc + 1) * 1024],
                                    ALU.mult, ALU.add),
                                waits=[(PE, last), (ACT, b2init_done[tc]), (DVE, wt_done[tc])],
                                incs=[(DVE, 1)])
                            stt_done[tc] = y_par_free[ypar]
                if DEBUG:
                    SQ.dma(lambda s, seq=seq, span=span: s.dma_start(dbg_wt[seq, span], wt_sp[:, span * 64:(span + 1) * 64]),
                           waits=[(DVE, max(wt_done))])
                    SQ.dma(lambda s, seq=seq, span=span: s.dma_start(dbg_y[seq, span], y_acc),
                           waits=[(DVE, max(stt_done.values()))])
                for tc in range(8):
                    trk[("g", seq, span, tc)] = bld.emit(
                        "scalar",
                        lambda s, tc=tc, span=span: s.activation(
                            g_park[:, (span * 8 + tc) * 1024: (span * 8 + tc + 1) * 1024],
                            y_acc[:, tc * 1024:(tc + 1) * 1024], AF.Silu),
                        waits=[(DVE, stt_done[tc])], incs=[(ACT, 1)])
            trk[("vt_pref", seq)] = GQ.dma(
                lambda g: g.dma_start(vt_ot.rearrange("p (cc h) -> p cc h", h=1024), vt_h.rearrange("(cc p) h -> p cc h", p=128)),
                waits=[(PE, trk.get(("oproj_last", seq - 1), 0))])
            trk[("g_done", seq)] = ACT.n
            if DEBUG:
                SQ.dma(lambda s, seq=seq: s.dma_start(dbg_g[seq], g_park[:, :]),
                       waits=[(ACT, trk[("g_done", seq)])])

        def phase_attn(seq):
            xn_ok = trk[("xn_done", seq)]
            v_vt = trk[("vt_pref", seq)]
            ones_ap = v_aug.rearrange("p (a b) -> p a b", b=65)[:, :, 64:65]
            v_ones = bld.emit("vector", lambda v: v.memset(ones_ap, 1.0), incs=[(DVE, 1)])
            vevac = {}
            vproj_last = 0
            for tkc in range(16):
                par = tkc % 2
                last = 0
                for cc in range(8):
                    for ch in range(2):
                        w = []
                        if cc == 0 and ch == 0:
                            w = [v_vt, (DVE, xn_ok), (ACT, vevac.get(par, 0))]
                        last = bld.emit(
                            "tensor",
                            lambda t, tkc=tkc, cc=cc, ch=ch, par=par: t.matmul(
                                psA[:, par * 1024 + ch * 512: par * 1024 + (ch + 1) * 512],
                                xn_fm[:, cc * 2048 + tkc * 128: cc * 2048 + (tkc + 1) * 128],
                                vt_ot[:, cc * 1024 + ch * 512: cc * 1024 + (ch + 1) * 512],
                                start=(cc == 0), stop=(cc == 7)),
                            waits=w,
                            incs=([(PE, 1)] if (cc == 7 and ch == 1) else []))
                vproj_last = last
                ev = 0
                for h in range(16):
                    ev = bld.emit(
                        "scalar",
                        lambda s, tkc=tkc, h=h, par=par: s.activation(
                            v_aug[:, tkc * 1040 + h * 65: tkc * 1040 + h * 65 + 64],
                            psA[:, par * 1024 + h * 64: par * 1024 + (h + 1) * 64], AF.Copy),
                        waits=([(PE, last), (DVE, v_ones)] if h == 0 else []),
                        incs=([(ACT, 1)] if h == 15 else []))
                vevac[par] = ev
            v_all_evac = max(vevac.values())
            qk_dma = {}
            qk_free = {}
            qk_evac = {}

            def dma_qk(hp):
                for i, wh in enumerate((qt_h, kt_h)):
                    slot = (hp % 2) * 2 + i
                    src = wh[:, hp * 128:(hp + 1) * 128].rearrange("(cc p) h -> p cc h", p=128)
                    dst = qk_w[:, slot * 1024:(slot + 1) * 1024].rearrange("p (cc h) -> p cc h", h=128)
                    qk_dma[(hp, i)] = GQ.dma(
                        lambda g, d=dst, s=src: g.dma_start(d, s),
                        waits=[(PE, qk_free.get(slot, vproj_last))])

            last_scores = [0]
            lane_exp = [0, 0, 0, 0]
            lane_avh = [[0, 0] for _ in range(4)]
            lane_step = [0, 0, 0, 0]
            lane_zevac = [(0, 0, None) for _ in range(4)]
            lane_sumtok = [None, None, None, None]
            dma_qk(0)
            for hp in range(8):
                if hp + 1 < 8:
                    dma_qk(hp + 1)
                for i, dstbuf in enumerate((q_hp, k_hp)):
                    slot = (hp % 2) * 2 + i
                    last = 0
                    for cc in range(8):
                        for tt in range(4):
                            w = []
                            if cc == 0 and tt == 0:
                                prev = (hp, 0) if i == 1 else (hp - 1, 1)
                                w = [qk_dma[(hp, i)], (DVE, xn_ok),
                                     (ACT, max(max(lane_exp), v_all_evac)),
                                     (DVE, qk_evac.get(prev, 0))]
                            last = bld.emit(
                                "tensor",
                                lambda t, slot=slot, cc=cc, tt=tt: t.matmul(
                                    psA[:, tt * 512:(tt + 1) * 512],
                                    qk_w[:, slot * 1024 + cc * 128: slot * 1024 + (cc + 1) * 128],
                                    xn_fm[:, cc * 2048 + tt * 512: cc * 2048 + (tt + 1) * 512],
                                    start=(cc == 0), stop=(cc == 7)),
                                waits=w,
                                incs=([(PE, 1)] if (cc == 7 and tt == 3) else []))
                    qk_free[slot] = last
                    qk_evac[(hp, i)] = bld.emit(
                        "vector",
                        lambda v, dstbuf=dstbuf: v.tensor_copy(dstbuf[:, :], psA[:, :]),
                        waits=[(PE, last)], incs=[(DVE, 1)])
                lanes_units = [[(0, 3), (0, 0)], [(1, 3), (1, 0)], [(0, 2), (0, 1)], [(1, 2), (1, 1)]]
                lstate = []
                for l in range(4):
                    lstate.append({"q": list(lanes_units[l]), "u": None, "t": 0, "pend": None})

                def emit_scores(l):
                    st_ = lstate[l]
                    h2u, vq = st_["u"]
                    h = hp * 2 + h2u
                    hbu = h2u * 64
                    tkc = st_["t"]
                    ntile = 4 * vq + 4
                    ga0 = 512 * vq
                    tk0 = tkc * 128
                    a = max(tk0, ga0)
                    b_ = ga0 + 512
                    la = a - ga0
                    lb = b_ - ga0
                    par = lane_step[l] % 2
                    w = [(ACT, lane_exp[l])]
                    if tkc == 0:
                        w.append((DVE, qk_evac[(hp, 1)]))
                    sc_v = bld.emit(
                        "tensor",
                        lambda t, l=l, la=la, lb=lb, a=a, b_=b_, tk0=tk0, hbu=hbu: t.matmul(
                            psA[:, l * 512 + la: l * 512 + lb],
                            k_hp[hbu:hbu + 64, tk0:tk0 + 128],
                            q_hp[hbu:hbu + 64, a:b_],
                            start=True, stop=True),
                        waits=w, incs=[(PE, 1)])
                    last_scores[0] = sc_v
                    prev_av = lane_avh[l][par]
                    lane_exp[l] = bld.emit(
                        "scalar",
                        lambda s, l=l, par=par, la=la, lb=lb: s.activation(
                            attnT[:, l * 1024 + par * 512 + la: l * 1024 + par * 512 + lb],
                            psA[:, l * 512 + la: l * 512 + lb], AF.Exp, bias=shift_sb[:, 0:1]),
                        waits=[(PE, sc_v), (PE, prev_av), consts_done_X],
                        incs=[(ACT, 1)])
                    mval = 0
                    if tk0 >= ga0:
                        mval = bld.emit(
                            "vector",
                            lambda v, l=l, par=par, la=la: v.tensor_tensor(
                                attnT[:, l * 1024 + par * 512 + la: l * 1024 + par * 512 + la + 128],
                                attnT[:, l * 1024 + par * 512 + la: l * 1024 + par * 512 + la + 128],
                                mask_bf[:, :], ALU.mult),
                            waits=[(ACT, lane_exp[l]), v_mask], incs=[(DVE, 1)])
                    st_["pend"] = (h2u, vq, tkc, par, la, lb, lane_exp[l], mval)
                    lane_step[l] += 1
                    st_["t"] += 1

                def emit_av(l):
                    st_ = lstate[l]
                    h2u, vq, tkc, par, la, lb, expv, mval = st_["pend"]
                    st_["pend"] = None
                    h = hp * 2 + h2u
                    ntile = 4 * vq + 4
                    w = [(ACT, expv)]
                    if mval:
                        w.append((DVE, mval))
                    if tkc == 0:
                        za, zd, zt = lane_zevac[l]
                        w += [(ACT, max(za, v_all_evac)), (DVE, zd)]
                        if zt is not None:
                            w.append(zt)
                    av_v = bld.emit(
                        "tensor",
                        lambda t, l=l, par=par, la=la, lb=lb, tkc=tkc, h=h, stop_b=(tkc == ntile - 1): t.matmul(
                            psB[0:65, l * 512 + la: l * 512 + lb],
                            v_aug[:, tkc * 1040 + h * 65: tkc * 1040 + h * 65 + 65],
                            attnT[:, l * 1024 + par * 512 + la: l * 1024 + par * 512 + lb],
                            start=(tkc == 0), stop=stop_b),
                        waits=w,
                        incs=[(PE, 1)])
                    lane_avh[l][par] = av_v
                    if tkc == ntile - 1:
                        _, _, prev_tok = lane_zevac[l]
                        if h2u == 0:
                            za = bld.emit(
                                "scalar",
                                lambda s, l=l, hp=hp, vq=vq: s.activation(
                                    z_park[0:64, hp * 2048 + 512 * vq: hp * 2048 + 512 * vq + 512],
                                    psB[0:64, l * 512:(l + 1) * 512], AF.Copy),
                                waits=[(PE, av_v)], incs=[(ACT, 1)])
                            ztok = None
                        else:
                            zw = [(PE, av_v)]
                            if prev_tok is not None:
                                zw.append(prev_tok)
                            za = bld.emit(
                                "scalar",
                                lambda s, l=l: s.activation(
                                    zstage[0:64, l * 512:(l + 1) * 512],
                                    psB[0:64, l * 512:(l + 1) * 512], AF.Copy),
                                waits=zw, incs=[(ACT, 1)])
                            ztok = SQ.dma(
                                lambda s, l=l, hp=hp, vq=vq: s.dma_start(
                                    z_park[64:128, hp * 2048 + 512 * vq: hp * 2048 + 512 * vq + 512],
                                    zstage[0:64, l * 512:(l + 1) * 512]),
                                waits=[(ACT, za)])
                        sw = [(PE, av_v), (ACT, za)]
                        if lane_sumtok[l] is not None:
                            sw.append(lane_sumtok[l])
                        zd = bld.emit(
                            "vector",
                            lambda v, l=l: v.tensor_copy(
                                srstage[64:65, l * 512:(l + 1) * 512],
                                psB[64:65, l * 512:(l + 1) * 512]),
                            waits=sw, incs=[(DVE, 1)])
                        lane_sumtok[l] = SQ.dma(
                            lambda s, l=l, h=h, vq=vq: s.dma_start(
                                sums[h:h + 1, 512 * vq: 512 * vq + 512],
                                srstage[64:65, l * 512:(l + 1) * 512]),
                            waits=[(DVE, zd)])
                        lane_zevac[l] = (za, zd, ztok)
                        st_["u"] = None

                active = True
                while active:
                    active = False
                    for l in range(4):
                        st_ = lstate[l]
                        if st_["pend"] is not None:
                            emit_av(l)
                            active = True
                        if st_["u"] is None:
                            if st_["q"]:
                                st_["u"] = st_["q"].pop(0)
                                st_["t"] = 0
                            else:
                                continue
                        if st_["t"] < 4 * st_["u"][1] + 4:
                            emit_scores(l)
                            active = True
            v_ot = GQ.dma(
                lambda g: g.dma_start(vt_ot.rearrange("p (cc h) -> p cc h", h=1024), ot_h.rearrange("(cc p) h -> p cc h", p=128)),
                waits=[(PE, last_scores[0])])
            rec = bld.emit(
                "vector",
                lambda v: v.reciprocal(srstage[0:16, :], sums[0:16, :]),
                waits=SQ.all_done(),
                incs=[(DVE, 1)])
            vdrain()
            recb = bld.emit(
                "vector", lambda v: v.tensor_copy(recip_bf[:, :], srstage[0:16, :]),
                waits=[(DVE, rec)], incs=[(DVE, 1)])
            rep_free = {0: 0, 1: 0}
            for hp in range(8):
                rp = psA if hp % 2 == 0 else psB
                last = 0
                for c0 in range(0, T, 512):
                    w = []
                    if c0 == 0:
                        w = [(DVE, recb), v_sel, (DVE, rep_free[hp % 2])]
                        if hp < 2:
                            for (za, zd, zt) in lane_zevac:
                                w.append((ACT, za))
                                if zt is not None:
                                    w.append(zt)
                    last = bld.emit(
                        "tensor",
                        lambda t, rp=rp, c0=c0, hp=hp: t.matmul(
                            rp[:, c0:c0 + 512],
                            sel2_bf[0:16, hp * 128:(hp + 1) * 128],
                            recip_bf[0:16, c0:c0 + 512],
                            start=True, stop=True),
                        waits=w,
                        incs=([(PE, 1)] if c0 == 1536 else []))
                rep_free[hp % 2] = bld.emit(
                    "vector",
                    lambda v, rp=rp, hp=hp: v.tensor_tensor(
                        z_park[:, hp * 2048:(hp + 1) * 2048],
                        z_park[:, hp * 2048:(hp + 1) * 2048],
                        rp[:, :], ALU.mult),
                    waits=[(PE, last)], incs=[(DVE, 1)])
            znorm_all = DVE.n
            if DEBUG:
                SQ.dma(lambda s, seq=seq: s.dma_start(dbg_z[seq], z_park[:, :]),
                       waits=[(DVE, znorm_all)])
            xdma = {}
            combine = {}
            odma = {}

            def dma_xa(tqc):
                par = tqc % 2
                w = [(PE, qk_free.get(2, 0)), (PE, qk_free.get(3, 0))]
                if tqc >= 2:
                    w.append((DVE, combine[tqc - 2]))
                xdma[tqc] = SQ.dma(
                    lambda s, tqc=tqc, par=par: s.dma_start(
                        x_att[:, par * 1024:(par + 1) * 1024],
                        x_hbm[seq, tqc * 128:(tqc + 1) * 128, :]),
                    waits=w)

            dma_xa(0)
            dma_xa(1)
            g_ok = trk[("g_done", seq)]
            for tqc in range(16):
                par = tqc % 2
                sp = psB[:, par * 1024:(par + 1) * 1024]
                last = 0
                for ac in range(8):
                    for ch in range(2):
                        w = []
                        if ac == 0 and ch == 0:
                            w = [(DVE, znorm_all), v_ot, (DVE, combine.get(tqc - 2, 0))]
                        last = bld.emit(
                            "tensor",
                            lambda t, sp=sp, ac=ac, ch=ch, tqc=tqc: t.matmul(
                                sp[:, ch * 512:(ch + 1) * 512],
                                z_park[:, ac * 2048 + tqc * 128: ac * 2048 + (tqc + 1) * 128],
                                vt_ot[:, ac * 1024 + ch * 512: ac * 1024 + (ch + 1) * 512],
                                start=(ac == 0), stop=(ac == 7)),
                            waits=w,
                            incs=([(PE, 1)] if (ac == 7 and ch == 1) else []))
                bld.emit(
                    "vector",
                    lambda v, sp=sp, tqc=tqc, par=par: v.tensor_tensor(
                        out_st[:, par * 1024:(par + 1) * 1024], sp,
                        g_park[:, tqc * 1024:(tqc + 1) * 1024], ALU.mult),
                    waits=[(PE, last), (ACT, g_ok)] + ([odma[tqc - 2]] if (tqc - 2) in odma else []))
                vdrain()
                combine[tqc] = bld.emit(
                    "vector",
                    lambda v, tqc=tqc, par=par: v.tensor_tensor(
                        out_st[:, par * 1024:(par + 1) * 1024],
                        out_st[:, par * 1024:(par + 1) * 1024],
                        x_att[:, par * 1024:(par + 1) * 1024], ALU.add),
                    waits=[xdma[tqc]], incs=[(DVE, 1)])
                odma[tqc] = SQ.dma(
                    lambda s, tqc=tqc, par=par: s.dma_start(
                        y_hbm[seq, tqc * 128:(tqc + 1) * 128, :],
                        out_st[:, par * 1024:(par + 1) * 1024]),
                    waits=[(DVE, combine[tqc])])
                if tqc + 2 < 16:
                    dma_xa(tqc + 2)
            trk[("oproj_last", seq)] = PE.n

        import os
        PH = os.environ.get("K_PHASES", "lfa")
        for seq in range(NSEQ):
            bld.barrier()
            phase_ln(seq)
            if "f" in PH:
                phase_ffn(seq)
            if "a" in PH:
                bld.barrier()
                phase_attn(seq)
        bld.emit("sync", lambda s: None, waits=SQ.all_done())

        @block.tensor
        def _(t):
            for fn in bld.q["tensor"]:
                fn(t)

        @block.scalar
        def _(s):
            for fn in bld.q["scalar"]:
                fn(s)

        @block.vector
        def _(v):
            for fn in bld.q["vector"]:
                fn(v)

        @block.sync
        def _(s):
            for fn in bld.q["sync"]:
                fn(s)

        @block.gpsimd
        def _(g):
            for fn in bld.q["gpsimd"]:
                fn(g)

    return nc


def make_in_maps(inputs):
    x = np.ascontiguousarray(np.asarray(inputs["L0_attn_x"], dtype=np.float32))
    Q = np.asarray(inputs["L0_attn_Q"], dtype=np.float32)
    K = np.asarray(inputs["L0_attn_K"], dtype=np.float32)
    V = np.asarray(inputs["L0_attn_V"], dtype=np.float32)
    Om = np.asarray(inputs["L0_attn_O"], dtype=np.float32)
    rw = np.asarray(inputs["L0_ffn_router_w"], dtype=np.float32)
    w1 = np.asarray(inputs["L0_ffn_w1"], dtype=np.float32)
    b1 = np.asarray(inputs["L0_ffn_b1"], dtype=np.float32)
    w2 = np.asarray(inputs["L0_ffn_w2"], dtype=np.float32)
    b2 = np.asarray(inputs["L0_ffn_b2"], dtype=np.float32)

    qt = np.ascontiguousarray(Q.reshape(1024, 1024).T)
    kt = np.ascontiguousarray(K.reshape(1024, 1024).T)
    vt = np.ascontiguousarray(V.reshape(1024, 1024).T)
    ot = np.ascontiguousarray(Om.T)
    rwt = np.ascontiguousarray(rw.T)
    w1t = np.ascontiguousarray(w1.transpose(0, 2, 1))
    w2t = np.ascontiguousarray(w2.transpose(0, 2, 1))
    b1r = np.ascontiguousarray(b1.reshape(8, 32, 128).transpose(2, 0, 1))
    ident = np.eye(128, dtype=np.float32)
    mask = np.triu(np.ones((128, 128), dtype=np.float32))
    sel2 = np.zeros((16, 1024), dtype=np.float32)
    for hp in range(8):
        sel2[2 * hp, hp * 128: hp * 128 + 64] = 1.0
        sel2[2 * hp + 1, hp * 128 + 64: (hp + 1) * 128] = 1.0

    shiftc = np.stack([np.full(128, -SHIFT, np.float32), np.full(128, -0.5, np.float32)], axis=1)
    shared = dict(qt=qt, kt=kt, vt=vt, ot=ot, rwt=rwt, w1t=w1t, w2t=w2t,
                  b1r=b1r, b2=np.ascontiguousarray(b2), ident=ident, mask=mask, sel2=sel2,
                  shiftc=shiftc)
    in_maps = []
    for c in range(8):
        m = dict(shared)
        m["x"] = np.ascontiguousarray(x[2 * c: 2 * c + 2])
        in_maps.append(m)
    return in_maps


def kernel(**inputs):
    nc = build_nc()
    in_maps = make_in_maps(inputs)
    res = run_bass_kernel_spmd(nc, in_maps, core_ids=list(range(8)))
    out = np.empty((16, T, D), np.float32)
    for c in range(8):
        out[2 * c: 2 * c + 2] = np.asarray(res.results[c]["y"])
    return out


# version 09
# speedup vs baseline: 1.1553x; 1.0124x over previous
import numpy as np
import concourse.bass as bass
import concourse.mybir as mybir
from concourse.bass_utils import run_bass_kernel_spmd

F32 = mybir.dt.float32
BF16 = mybir.dt.bfloat16
AF = mybir.ActivationFunctionType
ALU = mybir.AluOpType
AX = mybir.AxisListType

D = 1024
T = 2048
NSEQ = 2
NE = 8
FF = 4096
SPAN = 1024
NSPAN = T // SPAN
FC = 1024
NFC = FF // FC
NFSUB = FC // 128
SHIFT = 30.0
EPS = 1e-5

DEBUG = False

ENGS = ("tensor", "scalar", "vector", "sync", "gpsimd")


class Sem:
    __slots__ = ("h", "name", "n")

    def __init__(self, handle, name):
        self.h = handle
        self.name = name
        self.n = 0


class B:

    def __init__(self, sems):
        self.q = {e: [] for e in ENGS}
        self.sems = sems
        self.pending = {e: None for e in ENGS}

    def barrier(self):
        snap = [(s, s.n) for s in self.sems]
        for e in ENGS:
            self.pending[e] = list(snap)

    def emit(self, eng, fn, waits=(), incs=()):
        for (s, v) in waits:
            assert v <= s.n, f"wait-on-future {s.name}: {v} > {s.n}"
        bar = self.pending[eng]
        self.pending[eng] = None
        out_vals = []
        for (s, step) in incs:
            s.n += step
            out_vals.append(s.n)
        waits = tuple((s, v) for (s, v) in waits if v > 0)
        incs = tuple(incs)

        def run(e, bar=bar, waits=waits, incs=incs, fn=fn):
            if bar is not None:
                for (s, v) in bar:
                    if v > 0:
                        e.wait_ge(s.h, v)
            for (s, v) in waits:
                e.wait_ge(s.h, v)
            ins = fn(e)
            for (s, step) in incs:
                ins.then_inc(s.h, step)

        self.q[eng].append(run)
        if len(out_vals) == 1:
            return out_vals[0]
        return out_vals


class DmaQ:

    def __init__(self, bld, eng, sems):
        self.bld = bld
        self.eng = eng
        self.sems = sems
        self.i = 0

    def dma(self, fn, waits=()):
        s = self.sems[self.i % len(self.sems)]
        self.i += 1
        w = list(waits) + [(s, s.n)]
        v = self.bld.emit(self.eng, fn, waits=w, incs=[(s, 16)])
        return (s, v)

    def all_done(self):
        return [(s, s.n) for s in self.sems]


def build_nc():
    nc = bass.Bass("TRN2", target_bir_lowering=False)

    x_hbm = nc.dram_tensor("x", [NSEQ, T, D], F32, kind="ExternalInput").ap()
    qt_h = nc.dram_tensor("qt", [D, D], F32, kind="ExternalInput").ap()
    kt_h = nc.dram_tensor("kt", [D, D], F32, kind="ExternalInput").ap()
    vt_h = nc.dram_tensor("vt", [D, D], F32, kind="ExternalInput").ap()
    ot_h = nc.dram_tensor("ot", [D, D], F32, kind="ExternalInput").ap()
    rwt_h = nc.dram_tensor("rwt", [D, NE], F32, kind="ExternalInput").ap()
    w1t_h = nc.dram_tensor("w1t", [NE, D, FF], F32, kind="ExternalInput").ap()
    w2t_h = nc.dram_tensor("w2t", [NE, FF, D], F32, kind="ExternalInput").ap()
    b1r_h = nc.dram_tensor("b1r", [128, NE, 32], F32, kind="ExternalInput").ap()
    b2_h = nc.dram_tensor("b2", [NE, D], F32, kind="ExternalInput").ap()
    id_h = nc.dram_tensor("ident", [128, 128], F32, kind="ExternalInput").ap()
    mk_h = nc.dram_tensor("mask", [128, 128], F32, kind="ExternalInput").ap()
    sel_h = nc.dram_tensor("sel2", [16, 1024], F32, kind="ExternalInput").ap()
    shf_h = nc.dram_tensor("shiftc", [128, 1], F32, kind="ExternalInput").ap()
    y_hbm = nc.dram_tensor("y", [NSEQ, T, D], F32, kind="ExternalOutput").ap()
    if DEBUG:
        dbg_xn = nc.dram_tensor("dbg_xn", [NSEQ, 128, 16384], BF16, kind="ExternalOutput").ap()
        dbg_g = nc.dram_tensor("dbg_g", [NSEQ, 128, 16384], BF16, kind="ExternalOutput").ap()
        dbg_z = nc.dram_tensor("dbg_z", [NSEQ, 128, 16384], BF16, kind="ExternalOutput").ap()
        dbg_wt = nc.dram_tensor("dbg_wt", [NSEQ, NSPAN, 128, 64], F32, kind="ExternalOutput").ap()
        dbg_y = nc.dram_tensor("dbg_y", [NSEQ, NSPAN, 128, 8192], F32, kind="ExternalOutput").ap()

    from contextlib import ExitStack
    es = ExitStack()
    sb = lambda name, shape, dt: es.enter_context(nc.sbuf_tensor(name, shape, dt))
    ps = lambda name, shape, dt: es.enter_context(nc.psum_tensor(name, shape, dt))
    sem = lambda name: es.enter_context(nc.semaphore(name))
    with es:
        xn_fm = sb("xn_fm", [128, 16384], BF16)
        g_park = sb("g_park", [128, 16384], BF16)
        arena_b = sb("arena_b", [128, 50432], BF16)
        arena_f = sb("arena_f", [128, 8192], F32)
        ident_f = sb("ident_f", [128, 128], F32)
        mask_bf = sb("mask_bf", [128, 128], BF16)
        rwt_bf = sb("rwt_bf", [128, 64], BF16)
        b1_sb = sb("b1_sb", [128, 256], F32)
        b2_bf = sb("b2_bf", [8, 1024], BF16)
        wt_sp = sb("wt_sp", [128, 128], F32)
        wtt_sb = sb("wtt_sb", [8, 2048], BF16)
        rt_lg = sb("rt_lg", [128, 8], F32)
        rt_rank = sb("rt_rank", [128, 8], F32)
        rt_cmp = sb("rt_cmp", [128, 8], F32)
        rt_E = sb("rt_E", [128, 8], F32)
        rt_Em = sb("rt_Em", [128, 8], F32)
        rt_S = sb("rt_S", [128, 1], F32)
        rt_d = sb("rt_d", [128, 1], F32)
        rt_rcp = sb("rt_rcp", [128, 1], F32)
        st_sum = sb("st_sum", [128, 1], F32)
        st_ssq = sb("st_ssq", [128, 1], F32)
        st_ngm = sb("st_ngm", [128, 1], F32)
        st_var = sb("st_var", [128, 1], F32)
        st_tmp = sb("st_tmp", [128, 1], F32)
        st_rstd = sb("st_rstd", [128, 1], F32)
        shift_sb = sb("shift_sb", [128, 1], F32)
        psA = ps("psA", [128, 2048], F32)
        psB = ps("psB", [128, 2048], F32)
        _sg = [sem(f"sG{i}") for i in range(4)]
        _ss = [sem(f"sS{i}") for i in range(4)]
        _sPE = sem("sPE"); _sACT = sem("sACT"); _sDVE = sem("sDVE")
        block = es.enter_context(nc.Block())

        GQS = [Sem(h, f"G{i}") for i, h in enumerate(_sg)]
        SQS = [Sem(h, f"S{i}") for i, h in enumerate(_ss)]
        PE = Sem(_sPE, "PE")
        ACT = Sem(_sACT, "ACT")
        DVE = Sem(_sDVE, "DVE")
        bld = B(GQS + SQS + [PE, ACT, DVE])
        GQ = DmaQ(bld, "gpsimd", GQS)
        SQ = DmaQ(bld, "sync", SQS)

        def cb(col_off, parts, cols):
            return arena_b[0:parts, col_off: col_off + cols]

        def cf(col_off, parts, cols):
            return arena_f[0:parts, col_off: col_off + cols]

        y_acc = cf(0, 128, 8192)
        h_sb = cb(0, 128, 8192)
        w1r = cb(8192, 128, 8192)
        w2c = cb(16384, 128, 16384)
        x_ln = cf(0, 128, 2048)
        xn_tmp = cf(2048, 128, 2048)
        ln_sq = cf(4096, 128, 1024)
        v_aug = cb(0, 128, 16640)
        z_park = cb(16640, 128, 16384)
        vt_ot = cb(33024, 128, 8192)
        qk_w = cb(33024, 128, 4096)
        q_hp = cb(37120, 128, 2048)
        k_hp = cb(39168, 128, 2048)
        attnT = cb(41216, 128, 4096)
        recip_bf = cb(45312, 16, 2048)
        sel2_bf = cb(47360, 16, 1024)
        zstage = cb(48384, 64, 2048)
        sums = cf(0, 16, 2048)
        srstage = cf(2048, 128, 2048)
        x_att = cf(4096, 128, 2048)
        out_st = cf(6144, 128, 2048)

        v_ident = SQ.dma(lambda s: s.dma_start(ident_f[:, :], id_h[:, :]))
        v_mask = GQ.dma(lambda g: g.dma_start(mask_bf[:, :], mk_h[:, :]))
        v_sel = GQ.dma(lambda g: g.dma_start(sel2_bf[:, :], sel_h[:, :]))
        v_rwt = GQ.dma(
            lambda g: g.dma_start(rwt_bf.rearrange("p (cc e) -> p cc e", e=8), rwt_h.rearrange("(cc p) e -> p cc e", p=128)))
        v_b2 = GQ.dma(lambda g: g.dma_start(b2_bf[:, :], b2_h[:, :]))
        v_b1 = SQ.dma(
            lambda s: s.dma_start(b1_sb[:, :], b1r_h.rearrange("p e f -> p (e f)")))
        v_shf = SQ.dma(lambda s: s.dma_start(shift_sb[:, :], shf_h[:, :]))
        consts_done_X = v_shf

        def vdrain():
            bld.emit("vector", lambda v: v.drain())

        trk = {}

        def phase_ln(seq):
            xdma = [0] * 16
            trevac = {}
            xnw = [0] * 16
            lastT = [0] * 16
            rstdv = [0] * 16

            def dma_x(tc):
                par = tc % 2
                w = []
                if tc >= 2:
                    w = [(DVE, xnw[tc - 2])]
                src = x_hbm[seq, tc * 128:(tc + 1) * 128, :]
                dst = x_ln[:, par * 1024:(par + 1) * 1024]
                xdma[tc] = SQ.dma(lambda s, d=dst, sr=src: s.dma_start(d, sr), waits=w)

            dma_x(0)
            dma_x(1)
            for tc in range(16):
                par = tc % 2
                xt = x_ln[:, par * 1024:(par + 1) * 1024]
                xnt = xn_tmp[:, par * 1024:(par + 1) * 1024]
                bld.emit("vector",
                         lambda v, xt=xt: v.reduce_sum(st_sum[:, :], xt, axis=AX.X),
                         waits=[xdma[tc], (ACT, rstdv[tc - 1] if tc >= 1 else 0)])
                bld.emit("vector", lambda v, xt=xt: v.tensor_tensor(ln_sq[:, :], xt, xt, ALU.mult))
                vdrain()
                bld.emit("vector", lambda v: v.reduce_sum(st_ssq[:, :], ln_sq[:, :], axis=AX.X))
                bld.emit("vector", lambda v: v.tensor_scalar(st_ngm[:, :], st_sum[:, :], -1.0 / D, None, ALU.mult))
                vdrain()
                bld.emit("vector", lambda v: v.tensor_scalar(st_tmp[:, :], st_ssq[:, :], 1.0 / D, None, ALU.mult))
                vdrain()
                bld.emit("vector", lambda v: v.tensor_tensor(st_var[:, :], st_ngm[:, :], st_ngm[:, :], ALU.mult))
                vdrain()
                bld.emit("vector", lambda v: v.tensor_sub(st_var[:, :], st_tmp[:, :], st_var[:, :]))
                vdrain()
                vvar = bld.emit("vector", lambda v: v.tensor_scalar(st_var[:, :], st_var[:, :], EPS, None, ALU.add),
                                incs=[(DVE, 1)])
                lnv = bld.emit("scalar", lambda s: s.activation(st_tmp[:, :], st_var[:, :], AF.Ln),
                               waits=[(DVE, vvar)], incs=[(ACT, 1)])
                vml = bld.emit("vector", lambda v: v.tensor_scalar(st_tmp[:, :], st_tmp[:, :], -0.5, None, ALU.mult),
                               waits=[(ACT, lnv)], incs=[(DVE, 1)])
                rstdv[tc] = bld.emit(
                    "scalar", lambda s: s.activation(st_rstd[:, :], st_tmp[:, :], AF.Exp),
                    waits=[(DVE, vml)], incs=[(ACT, 1)])
                xnw[tc] = bld.emit(
                    "vector",
                    lambda v, xt=xt, xnt=xnt: v.tensor_scalar(xnt, xt, st_ngm[:, 0:1], st_rstd[:, 0:1], ALU.add, ALU.mult),
                    waits=[(ACT, rstdv[tc]), (PE, lastT[tc - 2] if tc >= 2 else 0)],
                    incs=[(DVE, 1)])
                for cc in range(8):
                    bank = (tc * 8 + cc) % 4
                    tdst = psA[:, bank * 512: bank * 512 + 128]
                    tsrc = xnt[:, cc * 128:(cc + 1) * 128]
                    lastT[tc] = bld.emit(
                        "tensor",
                        lambda t, d=tdst, s=tsrc: t.transpose(d, s, ident_f[:, :]),
                        waits=[(DVE, max(xnw[tc], trevac.get(bank, 0))), v_ident],
                        incs=[(PE, 1)])
                    edst = xn_fm[:, cc * 2048 + tc * 128: cc * 2048 + (tc + 1) * 128]
                    trevac[bank] = bld.emit(
                        "vector",
                        lambda v, d=edst, s=tdst: v.tensor_copy(d, s),
                        waits=[(PE, lastT[tc])],
                        incs=[(DVE, 1)])
                trk[("xn_tc", seq, tc)] = max(trevac.values())
                if tc + 2 < 16:
                    dma_x(tc + 2)
                if tc == 7:
                    emit_router(seq, 0)
                if tc == 15:
                    emit_router(seq, 1)
            trk[("xn_done", seq)] = max(trevac.values())
            if DEBUG:
                SQ.dma(lambda s, seq=seq: s.dma_start(dbg_xn[seq], xn_fm[:, :]),
                       waits=[(DVE, trk[("xn_done", seq)])])

        def emit_router(seq, span):
            sp0 = span * SPAN
            dveb_snap = max(DVE.n and trk.get(("router_psB_free", seq), 0), trk.get(("router_psB_free", seq), 0))
            lg_use = [0, 0]
            exp_done = [0, 0]
            wt_done = [0] * 8
            if True:
                for tc in range(8):
                    par = tc % 2
                    lg = psB[:, par * 512: par * 512 + 8]
                    last = 0
                    for cc in range(8):
                        w = []
                        if cc == 0:
                            w = [(DVE, trk[("xn_tc", seq, span * 8 + tc)]), v_rwt, (DVE, lg_use[par]), (DVE, dveb_snap)]
                        last = bld.emit(
                            "tensor",
                            lambda t, lg=lg, cc=cc, tc=tc, sp0=sp0: t.matmul(
                                lg,
                                xn_fm[:, cc * 2048 + sp0 + tc * 128: cc * 2048 + sp0 + (tc + 1) * 128],
                                rwt_bf[:, cc * 8:(cc + 1) * 8],
                                start=(cc == 0), stop=(cc == 7)),
                            waits=w,
                            incs=([(PE, 1)] if cc == 7 else []))
                    lg_use[par] = bld.emit(
                        "vector", lambda v, lg=lg: v.tensor_copy(rt_lg[:, :], lg),
                        waits=[(PE, last), (ACT, exp_done[0]), (ACT, exp_done[1])],
                        incs=[(DVE, 1)])
                    vdrain()
                    bld.emit("vector", lambda v: v.tensor_scalar(rt_rank[:, :], rt_lg[:, :], rt_lg[:, 0:1], None, ALU.is_lt))
                    for j in range(1, 8):
                        vdrain()
                        bld.emit("vector", lambda v, j=j: v.tensor_scalar(rt_cmp[:, :], rt_lg[:, :], rt_lg[:, j:j + 1], None, ALU.is_lt))
                        vdrain()
                        bld.emit("vector", lambda v: v.tensor_add(rt_rank[:, :], rt_rank[:, :], rt_cmp[:, :]))
                    for j in range(0, 7):
                        vdrain()
                        bld.emit("vector", lambda v, j=j: v.tensor_scalar(
                            rt_cmp[:, j + 1:8], rt_lg[:, j + 1:8], rt_lg[:, j:j + 1], None, ALU.is_equal))
                        vdrain()
                        bld.emit("vector", lambda v, j=j: v.tensor_add(rt_rank[:, j + 1:8], rt_rank[:, j + 1:8], rt_cmp[:, j + 1:8]))
                    vdrain()
                    bld.emit("vector", lambda v: v.tensor_scalar(rt_cmp[:, :], rt_rank[:, :], 1.5, None, ALU.is_le))
                    vdrain()
                    exp_done[par] = bld.emit(
                        "scalar", lambda s: s.activation(rt_E[:, :], rt_lg[:, :], AF.Exp),
                        waits=[(DVE, lg_use[par])], incs=[(ACT, 1)])
                    bld.emit("vector", lambda v: v.reduce_sum(rt_S[:, :], rt_E[:, :], axis=AX.X),
                             waits=[(ACT, exp_done[par])])
                    bld.emit("vector", lambda v: v.tensor_mul(rt_Em[:, :], rt_E[:, :], rt_cmp[:, :]))
                    vdrain()
                    bld.emit("vector", lambda v: v.reduce_sum(rt_d[:, :], rt_Em[:, :], axis=AX.X))
                    vdrain()
                    bld.emit("vector", lambda v: v.scalar_tensor_tensor(rt_d[:, :], rt_S[:, :], 1e-9, rt_d[:, :], ALU.mult, ALU.add))
                    vdrain()
                    bld.emit("vector", lambda v: v.reciprocal(rt_rcp[:, :], rt_d[:, :]))
                    vdrain()
                    wt_done[tc] = bld.emit(
                        "vector", lambda v, tc=tc, span=span: v.tensor_scalar(
                            wt_sp[:, span * 64 + tc * 8: span * 64 + (tc + 1) * 8], rt_Em[:, :], rt_rcp[:, 0:1], None, ALU.mult),
                        incs=[(DVE, 1)])

                lg_done = max(lg_use)
                wtt_done = [0] * 8
                for tc in range(8):
                    slot = tc % 2
                    tdst = psB[0:8, (2 + slot) * 512: (2 + slot) * 512 + 128]
                    w = [(DVE, wt_done[tc])]
                    if tc == 0:
                        w.append((DVE, dveb_snap))
                    if tc >= 2:
                        w.append((DVE, wtt_done[tc - 2]))
                    wtt_pe = bld.emit(
                        "tensor",
                        lambda t, d=tdst, tc=tc, span=span: t.transpose(
                            d, wt_sp[:, span * 64 + tc * 8: span * 64 + (tc + 1) * 8], ident_f[:, :]),
                        waits=w, incs=[(PE, 1)])
                    wtt_done[tc] = bld.emit(
                        "vector",
                        lambda v, d=tdst, tc=tc, span=span: v.tensor_copy(
                            wtt_sb[0:8, span * 1024 + tc * 128: span * 1024 + (tc + 1) * 128], d),
                        waits=[(PE, wtt_pe)], incs=[(DVE, 1)])
            trk[("router", seq, span)] = (wt_done, wtt_done, max(lg_use))
            trk[("router_psB_free", seq)] = DVE.n

        def phase_ffn(seq):
            w1_dma = trk.setdefault("w1_dma", {})
            w2_dma = trk.setdefault("w2_dma", {})
            w1_free = trk.setdefault("w1_free", {})
            w2_free = trk.setdefault("w2_free", {})
            h_slot_free = trk.setdefault("h_slot_free", {})
            y_par_free = trk.setdefault("y_par_free", {})
            router_dve = DVE.n

            for span in range(NSPAN):
                sp0 = span * SPAN
                dve_snap = DVE.n
                wt_done, wtt_done, lg_done = trk[("router", seq, span)]
                xn_ok = trk[("xn_tc", seq, span * 8 + 7)] if span == 0 else trk[("xn_tc", seq, 15)]
                wtt_last = max(wtt_done)
                b2init_done = [0] * 8
                for tc in range(8):
                    par = tc % 2
                    last = 0
                    for ch in range(2):
                        w = []
                        if ch == 0:
                            w = [(DVE, max(router_dve, dve_snap)), v_b2]
                            if tc >= 2:
                                w.append((ACT, b2init_done[tc - 2]))
                        last = bld.emit(
                            "tensor",
                            lambda t, tc=tc, ch=ch, par=par, span=span: t.matmul(
                                psB[:, par * 1024 + ch * 512: par * 1024 + (ch + 1) * 512],
                                wtt_sb[0:8, span * 1024 + tc * 128: span * 1024 + (tc + 1) * 128],
                                b2_bf[0:8, ch * 512:(ch + 1) * 512],
                                start=True, stop=True),
                            waits=w,
                            incs=([(PE, 1)] if ch == 1 else []))
                    b2init_done[tc] = bld.emit(
                        "scalar",
                        lambda s, tc=tc, par=par: s.activation(
                            y_acc[:, tc * 1024:(tc + 1) * 1024], psB[:, par * 1024:(par + 1) * 1024], AF.Copy),
                        waits=[(PE, last)], incs=[(ACT, 1)])

                h_evac = {}
                stt_done = {}

                def dma_w1(e, fc, ch2):
                    idx = (e * NFC + fc) * 2 + ch2
                    slot = idx % 2
                    f0 = fc * FC + ch2 * 512
                    src = w1t_h[e, :, f0:f0 + 512].rearrange("(cc p) f -> p cc f", p=128)
                    dst = w1r[:, slot * 4096:(slot + 1) * 4096].rearrange("p (cc f) -> p cc f", f=512)
                    w = [(PE, w1_free.get(slot, 0))]
                    w1_dma[(e, fc, ch2)] = GQ.dma(lambda g, d=dst, s=src: g.dma_start(d, s), waits=w)

                def dma_w2(e, fc):
                    idx = e * NFC + fc
                    slot = idx % 2
                    src = w2t_h[e, fc * FC:(fc + 1) * FC, :].rearrange("(fs p) c -> p fs c", p=128)
                    dst = w2c[:, slot * 8192:(slot + 1) * 8192].rearrange("p (fs c) -> p fs c", c=1024)
                    w = [(PE, w2_free.get(slot, 0))]
                    w2_dma[(e, fc)] = GQ.dma(lambda g, d=dst, s=src: g.dma_start(d, s), waits=w)

                dma_w1(0, 0, 0)
                dma_w1(0, 0, 1)
                dma_w2(0, 0)

                for e in range(NE):
                    for fc in range(NFC):
                        idx = e * NFC + fc
                        w2slot = idx % 2
                        nxt = idx + 1
                        if nxt < NE * NFC:
                            en, fcn = nxt // NFC, nxt % NFC
                            dma_w2(en, fcn)
                        for ch2 in range(2):
                            w1slot = ((e * NFC + fc) * 2 + ch2) % 2
                            for fl in range(4):
                                fs = ch2 * 4 + fl
                                for th in range(2):
                                    pslot = (fs * 2 + th) % 4
                                    pdst = psA[:, pslot * 512:(pslot + 1) * 512]
                                    last = 0
                                    for cc in range(8):
                                        w = []
                                        if cc == 0:
                                            w = [w1_dma[(e, fc, ch2)],
                                                 (ACT, h_slot_free.get(pslot, 0)),
                                                 (DVE, max(xn_ok, lg_done, wtt_last))]
                                        last = bld.emit(
                                            "tensor",
                                            lambda t, pdst=pdst, w1slot=w1slot, cc=cc, fl=fl, th=th, sp0=sp0: t.matmul(
                                                pdst,
                                                w1r[:, w1slot * 4096 + cc * 512 + fl * 128: w1slot * 4096 + cc * 512 + (fl + 1) * 128],
                                                xn_fm[:, cc * 2048 + sp0 + th * 512: cc * 2048 + sp0 + (th + 1) * 512],
                                                start=(cc == 0), stop=(cc == 7)),
                                            waits=w,
                                            incs=([(PE, 1)] if cc == 7 else []))
                                    w1_free[w1slot] = last
                                    bia = b1_sb[:, e * 32 + fc * 8 + fs: e * 32 + fc * 8 + fs + 1]
                                    h_slot_free[pslot] = bld.emit(
                                        "scalar",
                                        lambda s, fs=fs, th=th, pdst=pdst, bia=bia: s.activation(
                                            h_sb[:, fs * 1024 + th * 512: fs * 1024 + (th + 1) * 512],
                                            pdst, AF.Relu, bias=bia),
                                        waits=[(PE, last), v_b1],
                                        incs=[(ACT, 1)])
                                    h_evac[fs] = h_slot_free[pslot]
                        if nxt < NE * NFC:
                            dma_w1(en, fcn, 0)
                            dma_w1(en, fcn, 1)
                        for tc in range(8):
                            ypar = tc % 2
                            last = 0
                            for fs in range(NFSUB):
                                for ch in range(2):
                                    w = []
                                    if fs == 0 and ch == 0:
                                        w = [w2_dma[(e, fc)],
                                             (ACT, h_evac[NFSUB - 1]),
                                             (DVE, max(y_par_free.get(ypar, 0), dve_snap))]
                                    last = bld.emit(
                                        "tensor",
                                        lambda t, tc=tc, fs=fs, ch=ch, ypar=ypar, w2slot=w2slot: t.matmul(
                                            psB[:, ypar * 1024 + ch * 512: ypar * 1024 + (ch + 1) * 512],
                                            h_sb[:, fs * 1024 + tc * 128: fs * 1024 + (tc + 1) * 128],
                                            w2c[:, w2slot * 8192 + fs * 1024 + ch * 512: w2slot * 8192 + fs * 1024 + (ch + 1) * 512],
                                            start=(fs == 0), stop=(fs == NFSUB - 1)),
                                        waits=w,
                                        incs=([(PE, 1)] if (fs == NFSUB - 1 and ch == 1) else []))
                            w2_free[w2slot] = last
                            y_par_free[ypar] = bld.emit(
                                "vector",
                                lambda v, tc=tc, ypar=ypar, e=e, span=span: v.scalar_tensor_tensor(
                                    y_acc[:, tc * 1024:(tc + 1) * 1024],
                                    psB[:, ypar * 1024:(ypar + 1) * 1024],
                                    wt_sp[:, span * 64 + tc * 8 + e: span * 64 + tc * 8 + e + 1],
                                    y_acc[:, tc * 1024:(tc + 1) * 1024],
                                    ALU.mult, ALU.add),
                                waits=[(PE, last), (ACT, b2init_done[tc]), (DVE, wt_done[tc])],
                                incs=[(DVE, 1)])
                            stt_done[tc] = y_par_free[ypar]
                if DEBUG:
                    SQ.dma(lambda s, seq=seq, span=span: s.dma_start(dbg_wt[seq, span], wt_sp[:, span * 64:(span + 1) * 64]),
                           waits=[(DVE, max(wt_done))])
                    SQ.dma(lambda s, seq=seq, span=span: s.dma_start(dbg_y[seq, span], y_acc),
                           waits=[(DVE, max(stt_done.values()))])
                for tc in range(8):
                    trk[("g", seq, span, tc)] = bld.emit(
                        "scalar",
                        lambda s, tc=tc, span=span: s.activation(
                            g_park[:, (span * 8 + tc) * 1024: (span * 8 + tc + 1) * 1024],
                            y_acc[:, tc * 1024:(tc + 1) * 1024], AF.Silu),
                        waits=[(DVE, stt_done[tc])], incs=[(ACT, 1)])
            trk[("g_done", seq)] = ACT.n
            if DEBUG:
                SQ.dma(lambda s, seq=seq: s.dma_start(dbg_g[seq], g_park[:, :]),
                       waits=[(ACT, trk[("g_done", seq)])])

        def phase_attn(seq):
            xn_ok = trk[("xn_done", seq)]
            v_vt = GQ.dma(
                lambda g: g.dma_start(vt_ot.rearrange("p (cc h) -> p cc h", h=1024), vt_h.rearrange("(cc p) h -> p cc h", p=128)))
            ones_ap = v_aug.rearrange("p (a b) -> p a b", b=65)[:, :, 64:65]
            v_ones = bld.emit("vector", lambda v: v.memset(ones_ap, 1.0), incs=[(DVE, 1)])
            vevac = {}
            vproj_last = 0
            for tkc in range(16):
                par = tkc % 2
                last = 0
                for cc in range(8):
                    for ch in range(2):
                        w = []
                        if cc == 0 and ch == 0:
                            w = [v_vt, (DVE, xn_ok), (ACT, vevac.get(par, 0))]
                        last = bld.emit(
                            "tensor",
                            lambda t, tkc=tkc, cc=cc, ch=ch, par=par: t.matmul(
                                psA[:, par * 1024 + ch * 512: par * 1024 + (ch + 1) * 512],
                                xn_fm[:, cc * 2048 + tkc * 128: cc * 2048 + (tkc + 1) * 128],
                                vt_ot[:, cc * 1024 + ch * 512: cc * 1024 + (ch + 1) * 512],
                                start=(cc == 0), stop=(cc == 7)),
                            waits=w,
                            incs=([(PE, 1)] if (cc == 7 and ch == 1) else []))
                vproj_last = last
                ev = 0
                for h in range(16):
                    ev = bld.emit(
                        "scalar",
                        lambda s, tkc=tkc, h=h, par=par: s.activation(
                            v_aug[:, tkc * 1040 + h * 65: tkc * 1040 + h * 65 + 64],
                            psA[:, par * 1024 + h * 64: par * 1024 + (h + 1) * 64], AF.Copy),
                        waits=([(PE, last), (DVE, v_ones)] if h == 0 else []),
                        incs=([(ACT, 1)] if h == 15 else []))
                vevac[par] = ev
            v_all_evac = max(vevac.values())
            qk_dma = {}
            qk_free = {}
            qk_evac = {}

            def dma_qk(hp):
                for i, wh in enumerate((qt_h, kt_h)):
                    slot = (hp % 2) * 2 + i
                    src = wh[:, hp * 128:(hp + 1) * 128].rearrange("(cc p) h -> p cc h", p=128)
                    dst = qk_w[:, slot * 1024:(slot + 1) * 1024].rearrange("p (cc h) -> p cc h", h=128)
                    qk_dma[(hp, i)] = GQ.dma(
                        lambda g, d=dst, s=src: g.dma_start(d, s),
                        waits=[(PE, qk_free.get(slot, vproj_last))])

            last_scores = [0]
            lane_exp = [0, 0, 0, 0]
            lane_avh = [[0, 0] for _ in range(4)]
            lane_step = [0, 0, 0, 0]
            lane_zevac = [(0, 0, None) for _ in range(4)]
            lane_sumtok = [None, None, None, None]
            dma_qk(0)
            for hp in range(8):
                if hp + 1 < 8:
                    dma_qk(hp + 1)
                for i, dstbuf in enumerate((q_hp, k_hp)):
                    slot = (hp % 2) * 2 + i
                    last = 0
                    for cc in range(8):
                        for tt in range(4):
                            w = []
                            if cc == 0 and tt == 0:
                                prev = (hp, 0) if i == 1 else (hp - 1, 1)
                                w = [qk_dma[(hp, i)], (DVE, xn_ok),
                                     (ACT, max(max(lane_exp), v_all_evac)),
                                     (DVE, qk_evac.get(prev, 0))]
                            last = bld.emit(
                                "tensor",
                                lambda t, slot=slot, cc=cc, tt=tt: t.matmul(
                                    psA[:, tt * 512:(tt + 1) * 512],
                                    qk_w[:, slot * 1024 + cc * 128: slot * 1024 + (cc + 1) * 128],
                                    xn_fm[:, cc * 2048 + tt * 512: cc * 2048 + (tt + 1) * 512],
                                    start=(cc == 0), stop=(cc == 7)),
                                waits=w,
                                incs=([(PE, 1)] if (cc == 7 and tt == 3) else []))
                    qk_free[slot] = last
                    qk_evac[(hp, i)] = bld.emit(
                        "vector",
                        lambda v, dstbuf=dstbuf: v.tensor_copy(dstbuf[:, :], psA[:, :]),
                        waits=[(PE, last)], incs=[(DVE, 1)])
                lanes_units = [[(0, 3), (0, 0)], [(1, 3), (1, 0)], [(0, 2), (0, 1)], [(1, 2), (1, 1)]]
                lstate = []
                for l in range(4):
                    lstate.append({"q": list(lanes_units[l]), "u": None, "t": 0, "pend": None})

                def emit_scores(l):
                    st_ = lstate[l]
                    h2u, vq = st_["u"]
                    h = hp * 2 + h2u
                    hbu = h2u * 64
                    tkc = st_["t"]
                    ntile = 4 * vq + 4
                    ga0 = 512 * vq
                    tk0 = tkc * 128
                    a = max(tk0, ga0)
                    b_ = ga0 + 512
                    la = a - ga0
                    lb = b_ - ga0
                    par = lane_step[l] % 2
                    w = [(ACT, lane_exp[l])]
                    if tkc == 0:
                        w.append((DVE, qk_evac[(hp, 1)]))
                    sc_v = bld.emit(
                        "tensor",
                        lambda t, l=l, la=la, lb=lb, a=a, b_=b_, tk0=tk0, hbu=hbu: t.matmul(
                            psA[:, l * 512 + la: l * 512 + lb],
                            k_hp[hbu:hbu + 64, tk0:tk0 + 128],
                            q_hp[hbu:hbu + 64, a:b_],
                            start=True, stop=True),
                        waits=w, incs=[(PE, 1)])
                    last_scores[0] = sc_v
                    prev_av = lane_avh[l][par]
                    lane_exp[l] = bld.emit(
                        "scalar",
                        lambda s, l=l, par=par, la=la, lb=lb: s.activation(
                            attnT[:, l * 1024 + par * 512 + la: l * 1024 + par * 512 + lb],
                            psA[:, l * 512 + la: l * 512 + lb], AF.Exp, bias=shift_sb[:, 0:1]),
                        waits=[(PE, sc_v), (PE, prev_av), consts_done_X],
                        incs=[(ACT, 1)])
                    mval = 0
                    if tk0 >= ga0:
                        mval = bld.emit(
                            "vector",
                            lambda v, l=l, par=par, la=la: v.tensor_tensor(
                                attnT[:, l * 1024 + par * 512 + la: l * 1024 + par * 512 + la + 128],
                                attnT[:, l * 1024 + par * 512 + la: l * 1024 + par * 512 + la + 128],
                                mask_bf[:, :], ALU.mult),
                            waits=[(ACT, lane_exp[l]), v_mask], incs=[(DVE, 1)])
                    st_["pend"] = (h2u, vq, tkc, par, la, lb, lane_exp[l], mval)
                    lane_step[l] += 1
                    st_["t"] += 1

                def emit_av(l):
                    st_ = lstate[l]
                    h2u, vq, tkc, par, la, lb, expv, mval = st_["pend"]
                    st_["pend"] = None
                    h = hp * 2 + h2u
                    ntile = 4 * vq + 4
                    w = [(ACT, expv)]
                    if mval:
                        w.append((DVE, mval))
                    if tkc == 0:
                        za, zd, zt = lane_zevac[l]
                        w += [(ACT, max(za, v_all_evac)), (DVE, zd)]
                        if zt is not None:
                            w.append(zt)
                    av_v = bld.emit(
                        "tensor",
                        lambda t, l=l, par=par, la=la, lb=lb, tkc=tkc, h=h, stop_b=(tkc == ntile - 1): t.matmul(
                            psB[0:65, l * 512 + la: l * 512 + lb],
                            v_aug[:, tkc * 1040 + h * 65: tkc * 1040 + h * 65 + 65],
                            attnT[:, l * 1024 + par * 512 + la: l * 1024 + par * 512 + lb],
                            start=(tkc == 0), stop=stop_b),
                        waits=w,
                        incs=[(PE, 1)])
                    lane_avh[l][par] = av_v
                    if tkc == ntile - 1:
                        _, _, prev_tok = lane_zevac[l]
                        if h2u == 0:
                            za = bld.emit(
                                "scalar",
                                lambda s, l=l, hp=hp, vq=vq: s.activation(
                                    z_park[0:64, hp * 2048 + 512 * vq: hp * 2048 + 512 * vq + 512],
                                    psB[0:64, l * 512:(l + 1) * 512], AF.Copy),
                                waits=[(PE, av_v)], incs=[(ACT, 1)])
                            ztok = None
                        else:
                            zw = [(PE, av_v)]
                            if prev_tok is not None:
                                zw.append(prev_tok)
                            za = bld.emit(
                                "scalar",
                                lambda s, l=l: s.activation(
                                    zstage[0:64, l * 512:(l + 1) * 512],
                                    psB[0:64, l * 512:(l + 1) * 512], AF.Copy),
                                waits=zw, incs=[(ACT, 1)])
                            ztok = SQ.dma(
                                lambda s, l=l, hp=hp, vq=vq: s.dma_start(
                                    z_park[64:128, hp * 2048 + 512 * vq: hp * 2048 + 512 * vq + 512],
                                    zstage[0:64, l * 512:(l + 1) * 512]),
                                waits=[(ACT, za)])
                        sw = [(PE, av_v), (ACT, za)]
                        if lane_sumtok[l] is not None:
                            sw.append(lane_sumtok[l])
                        zd = bld.emit(
                            "vector",
                            lambda v, l=l: v.tensor_copy(
                                srstage[64:65, l * 512:(l + 1) * 512],
                                psB[64:65, l * 512:(l + 1) * 512]),
                            waits=sw, incs=[(DVE, 1)])
                        lane_sumtok[l] = SQ.dma(
                            lambda s, l=l, h=h, vq=vq: s.dma_start(
                                sums[h:h + 1, 512 * vq: 512 * vq + 512],
                                srstage[64:65, l * 512:(l + 1) * 512]),
                            waits=[(DVE, zd)])
                        lane_zevac[l] = (za, zd, ztok)
                        st_["u"] = None

                active = True
                while active:
                    active = False
                    for l in range(4):
                        st_ = lstate[l]
                        if st_["pend"] is not None:
                            emit_av(l)
                            active = True
                        if st_["u"] is None:
                            if st_["q"]:
                                st_["u"] = st_["q"].pop(0)
                                st_["t"] = 0
                            else:
                                continue
                        if st_["t"] < 4 * st_["u"][1] + 4:
                            emit_scores(l)
                            active = True
            v_ot = GQ.dma(
                lambda g: g.dma_start(vt_ot.rearrange("p (cc h) -> p cc h", h=1024), ot_h.rearrange("(cc p) h -> p cc h", p=128)),
                waits=[(PE, last_scores[0])])
            rec = bld.emit(
                "vector",
                lambda v: v.reciprocal(srstage[0:16, :], sums[0:16, :]),
                waits=SQ.all_done(),
                incs=[(DVE, 1)])
            vdrain()
            recb = bld.emit(
                "vector", lambda v: v.tensor_copy(recip_bf[:, :], srstage[0:16, :]),
                waits=[(DVE, rec)], incs=[(DVE, 1)])
            rep_free = {0: 0, 1: 0}
            for hp in range(8):
                rp = psA if hp % 2 == 0 else psB
                last = 0
                for c0 in range(0, T, 512):
                    w = []
                    if c0 == 0:
                        w = [(DVE, recb), v_sel, (DVE, rep_free[hp % 2])]
                        if hp < 2:
                            for (za, zd, zt) in lane_zevac:
                                w.append((ACT, za))
                                if zt is not None:
                                    w.append(zt)
                    last = bld.emit(
                        "tensor",
                        lambda t, rp=rp, c0=c0, hp=hp: t.matmul(
                            rp[:, c0:c0 + 512],
                            sel2_bf[0:16, hp * 128:(hp + 1) * 128],
                            recip_bf[0:16, c0:c0 + 512],
                            start=True, stop=True),
                        waits=w,
                        incs=([(PE, 1)] if c0 == 1536 else []))
                rep_free[hp % 2] = bld.emit(
                    "vector",
                    lambda v, rp=rp, hp=hp: v.tensor_tensor(
                        z_park[:, hp * 2048:(hp + 1) * 2048],
                        z_park[:, hp * 2048:(hp + 1) * 2048],
                        rp[:, :], ALU.mult),
                    waits=[(PE, last)], incs=[(DVE, 1)])
            znorm_all = DVE.n
            if DEBUG:
                SQ.dma(lambda s, seq=seq: s.dma_start(dbg_z[seq], z_park[:, :]),
                       waits=[(DVE, znorm_all)])
            xdma = {}
            combine = {}
            odma = {}

            def dma_xa(tqc):
                par = tqc % 2
                w = [(PE, qk_free.get(2, 0)), (PE, qk_free.get(3, 0)),
                     (ACT, trk[("g_done", seq)])]
                if tqc >= 2:
                    w.append((DVE, combine[tqc - 2]))
                xdma[tqc] = SQ.dma(
                    lambda s, tqc=tqc, par=par: s.dma_start(
                        x_att[:, par * 1024:(par + 1) * 1024],
                        x_hbm[seq, tqc * 128:(tqc + 1) * 128, :]),
                    waits=w)

            dma_xa(0)
            dma_xa(1)
            g_ok = trk[("g_done", seq)]
            for tqc in range(16):
                par = tqc % 2
                sp = psB[:, par * 1024:(par + 1) * 1024]
                last = 0
                for ac in range(8):
                    for ch in range(2):
                        w = []
                        if ac == 0 and ch == 0:
                            w = [(DVE, znorm_all), v_ot, (DVE, combine.get(tqc - 2, 0))]
                        last = bld.emit(
                            "tensor",
                            lambda t, sp=sp, ac=ac, ch=ch, tqc=tqc: t.matmul(
                                sp[:, ch * 512:(ch + 1) * 512],
                                z_park[:, ac * 2048 + tqc * 128: ac * 2048 + (tqc + 1) * 128],
                                vt_ot[:, ac * 1024 + ch * 512: ac * 1024 + (ch + 1) * 512],
                                start=(ac == 0), stop=(ac == 7)),
                            waits=w,
                            incs=([(PE, 1)] if (ac == 7 and ch == 1) else []))
                bld.emit(
                    "vector",
                    lambda v, sp=sp, tqc=tqc, par=par: v.tensor_tensor(
                        out_st[:, par * 1024:(par + 1) * 1024], sp,
                        g_park[:, tqc * 1024:(tqc + 1) * 1024], ALU.mult),
                    waits=[(PE, last), (ACT, g_ok)] + ([odma[tqc - 2]] if (tqc - 2) in odma else []))
                vdrain()
                combine[tqc] = bld.emit(
                    "vector",
                    lambda v, tqc=tqc, par=par: v.tensor_tensor(
                        out_st[:, par * 1024:(par + 1) * 1024],
                        out_st[:, par * 1024:(par + 1) * 1024],
                        x_att[:, par * 1024:(par + 1) * 1024], ALU.add),
                    waits=[xdma[tqc]], incs=[(DVE, 1)])
                odma[tqc] = SQ.dma(
                    lambda s, tqc=tqc, par=par: s.dma_start(
                        y_hbm[seq, tqc * 128:(tqc + 1) * 128, :],
                        out_st[:, par * 1024:(par + 1) * 1024]),
                    waits=[(DVE, combine[tqc])])
                if tqc + 2 < 16:
                    dma_xa(tqc + 2)

        import os
        PH = os.environ.get("K_PHASES", "lfa")
        for seq in range(NSEQ):
            bld.barrier()
            phase_ln(seq)
            if "f" in PH:
                phase_ffn(seq)
            if "a" in PH:
                bld.barrier()
                phase_attn(seq)
        bld.emit("sync", lambda s: None, waits=SQ.all_done())

        @block.tensor
        def _(t):
            for fn in bld.q["tensor"]:
                fn(t)

        @block.scalar
        def _(s):
            for fn in bld.q["scalar"]:
                fn(s)

        @block.vector
        def _(v):
            for fn in bld.q["vector"]:
                fn(v)

        @block.sync
        def _(s):
            for fn in bld.q["sync"]:
                fn(s)

        @block.gpsimd
        def _(g):
            for fn in bld.q["gpsimd"]:
                fn(g)

    return nc


def make_in_maps(inputs):
    x = np.ascontiguousarray(np.asarray(inputs["L0_attn_x"], dtype=np.float32))
    Q = np.asarray(inputs["L0_attn_Q"], dtype=np.float32)
    K = np.asarray(inputs["L0_attn_K"], dtype=np.float32)
    V = np.asarray(inputs["L0_attn_V"], dtype=np.float32)
    Om = np.asarray(inputs["L0_attn_O"], dtype=np.float32)
    rw = np.asarray(inputs["L0_ffn_router_w"], dtype=np.float32)
    w1 = np.asarray(inputs["L0_ffn_w1"], dtype=np.float32)
    b1 = np.asarray(inputs["L0_ffn_b1"], dtype=np.float32)
    w2 = np.asarray(inputs["L0_ffn_w2"], dtype=np.float32)
    b2 = np.asarray(inputs["L0_ffn_b2"], dtype=np.float32)

    qt = np.ascontiguousarray(Q.reshape(1024, 1024).T)
    kt = np.ascontiguousarray(K.reshape(1024, 1024).T)
    vt = np.ascontiguousarray(V.reshape(1024, 1024).T)
    ot = np.ascontiguousarray(Om.T)
    rwt = np.ascontiguousarray(rw.T)
    w1t = np.ascontiguousarray(w1.transpose(0, 2, 1))
    w2t = np.ascontiguousarray(w2.transpose(0, 2, 1))
    b1r = np.ascontiguousarray(b1.reshape(8, 32, 128).transpose(2, 0, 1))
    ident = np.eye(128, dtype=np.float32)
    mask = np.triu(np.ones((128, 128), dtype=np.float32))
    sel2 = np.zeros((16, 1024), dtype=np.float32)
    for hp in range(8):
        sel2[2 * hp, hp * 128: hp * 128 + 64] = 1.0
        sel2[2 * hp + 1, hp * 128 + 64: (hp + 1) * 128] = 1.0

    shiftc = np.full((128, 1), -SHIFT, dtype=np.float32)
    shared = dict(qt=qt, kt=kt, vt=vt, ot=ot, rwt=rwt, w1t=w1t, w2t=w2t,
                  b1r=b1r, b2=np.ascontiguousarray(b2), ident=ident, mask=mask, sel2=sel2,
                  shiftc=shiftc)
    in_maps = []
    for c in range(8):
        m = dict(shared)
        m["x"] = np.ascontiguousarray(x[2 * c: 2 * c + 2])
        in_maps.append(m)
    return in_maps


def kernel(**inputs):
    nc = build_nc()
    in_maps = make_in_maps(inputs)
    res = run_bass_kernel_spmd(nc, in_maps, core_ids=list(range(8)))
    out = np.empty((16, T, D), np.float32)
    for c in range(8):
        out[2 * c: 2 * c + 2] = np.asarray(res.results[c]["y"])
    return out
